# version 07
# baseline (speedup 1.0000x reference)
import numpy as np
import concourse.bass as bass
import concourse.mybir as mybir
from concourse.bass_utils import run_bass_kernel_spmd

F32 = mybir.dt.float32
BF16 = mybir.dt.bfloat16
I32 = mybir.dt.int32
AF = mybir.ActivationFunctionType
AX = mybir.AxisListType.X

DEBUG_STAGE = 5
ATTN_HEADS = 8
ATTN_GQ = 16
ATTN_PART = 9
WARMERS = False

T_LOC = 2048
C = 1024
H = 4096
NTC = 16
EPS = 1e-5

A_XQK = 0
A_XV = 32768
A_Y = 49408
ARENA_COLS = 65792
PT_R1, PT_R2, PT_R3 = 0, 2048, 4096


def build():
    nc = bass.Bass("TRN2", target_bir_lowering=False,
                   detect_race_conditions=(DEBUG_STAGE < 4))

    def din(name, shape, dt=F32):
        return nc.dram_tensor(name, list(shape), dt, kind="ExternalInput").ap()

    x0 = din("x0", [T_LOC, C])
    w1t = din("w1t", [32, 128, C]); w2t = din("w2t", [32, 128, C])
    h1t = din("h1t", [32, 128, H]); h2t = din("h2t", [32, 128, H])
    w3t = din("w3t", [H, C])
    wqkv = din("wqkv", [16, 128, C])
    wqkvv = din("wqkvv", [8, 128, C])
    wproj = din("wproj", [8, 128, C])
    s1d = din("s1", [128, 32]); s2d = din("s2", [128, 32])
    hs1d = din("hs1", [128, 32]); hs2d = din("hs2", [128, 32])
    s3rowd = din("s3row", [1, C])
    bqkvd = din("bqkv", [128, 24])
    bqkvrowd = din("bqkvrow", [1, 3 * C])
    bprojrowd = din("bprojrow", [1, C])
    mnsd = din("mns", [128, 2048]); mstd = din("mst", [128, 2048])
    eyed = din("eye", [128, 128])
    eye64d = din("eye64", [128, 64])
    nonced = din("nonce", [1, 2], I32)

    yout = nc.dram_tensor("yout", [T_LOC, C], F32, kind="ExternalOutput").ap()
    dbg = {}
    if DEBUG_STAGE == 1:
        dbg["u1"] = nc.dram_tensor("dbg_u1", [128, 8 * T_LOC], F32,
                                   kind="ExternalOutput").ap()
    if DEBUG_STAGE == 3:
        dbg["y"] = nc.dram_tensor("dbg_y", [128, 4 * 4096], F32,
                                  kind="ExternalOutput").ap()
        dbg["xp"] = nc.dram_tensor("dbg_xp", [128, 4 * 2048], F32,
                                   kind="ExternalOutput").ap()

    x1_d = nc.dram_tensor("x1_d", [T_LOC, C], F32, kind="Internal").ap()
    x2_d = nc.dram_tensor("x2_d", [T_LOC, C], F32, kind="Internal").ap()
    h2_d = nc.dram_tensor("h2_d", [32, 4, 128, T_LOC // 4], BF16,
                          kind="Internal").ap()
    h3_d = nc.dram_tensor("h3_d", [32, 4, 128, T_LOC // 4], BF16,
                          kind="Internal").ap()
    sh_y = nc.dram_tensor("sh_y", [2, 4, 128, 2048], BF16, kind="Internal",
                          addr_space="Shared").ap()
    sh_flag = nc.dram_tensor("sh_flag", [2, 1, 2], I32, kind="Internal",
                             addr_space="Shared").ap()

    from contextlib import ExitStack
    ctx = ExitStack()
    sbt = lambda n, s, d: ctx.enter_context(nc.sbuf_tensor(n, s, d))
    pst = lambda n, s, d: ctx.enter_context(nc.psum_tensor(n, s, d))

    AR = sbt("AR", [128, ARENA_COLS], BF16)
    BU = sbt("BU", [128, 8 * T_LOC], BF16)
    WB = sbt("WB", [128, 8192], BF16)
    MK = sbt("MK", [128, 4096], BF16)
    XB = sbt("XB", [128, 2048], F32)
    PT = WB
    UT = sbt("UT", [128, 2048], BF16)
    STt = sbt("STt", [128, 64], F32)
    s1t = sbt("s1t", [128, 32], F32); s2t = sbt("s2t", [128, 32], F32)
    hs1t = sbt("hs1t", [128, 32], F32); hs2t = sbt("hs2t", [128, 32], F32)
    bqkvt = sbt("bqkvt", [128, 24], F32)
    rowbuf = sbt("rowbuf", [1, C], BF16)
    s3bc = sbt("s3bc", [128, C], BF16)
    vbias = sbt("vbias", [128, C], BF16)
    eyet = sbt("eyet", [128, 128], BF16)
    eye64t = sbt("eye64t", [128, 64], BF16)
    qstage = sbt("qstage", [128, 256], BF16)
    ones1 = sbt("ones1", [1, 128], BF16)
    ones64b = sbt("ones64b", [128, 64], F32)
    onescol = sbt("onescol", [128, 1], BF16)
    ripr2 = sbt("ripr2", [128, 256], F32)
    ripb = sbt("ripb", [64, 256], F32)
    ytmp = sbt("ytmp", [128, 256], BF16)
    noncet = sbt("noncet", [1, 2], I32)
    flagt = sbt("flagt", [1, 2], I32)

    psA = pst("psA", [128, 2048], F32)
    psB = pst("psB", [128, 1024], F32)
    psD = pst("psD", [128, 512], F32)
    psE = pst("psE", [128, 512], BF16)

    sems = {}
    cnt = {}
    for s in ["SW", "FW", "SA", "FA", "SM", "FP", "FE", "SE", "SV", "SS",
              "SG", "PS", "SW0", "SW1", "SB0", "SB1", "SX0", "SX1", "SH",
              "SHG",
              "SU0", "SU1", "SU2", "SU3", "SD0", "SD1", "FPV", "FPS"]:
        sems[s] = ctx.enter_context(nc.semaphore(s))
        cnt[s] = 0

    blk_guard = nc.Block()
    blk_guard.__enter__()
    g = nc.gpsimd; sy = nc.sync; te = nc.tensor; sc = nc.scalar; ve = nc.vector

    def I(instr, s, n):
        cnt[s] += n
        instr.then_inc(sems[s], n)
        return cnt[s]

    def W(eng, s, val=None):
        v = cnt[s] if val is None else val
        if v > 0:
            eng.wait_ge(sems[s], v)

    def temark(s):
        return I(te.nop(), s, 1)

    def WFP(eng):
        W(eng, "FPV"); W(eng, "FPS")

    def warm(pool):
        i1, p1, o1, f1 = pool.take()
        W(te, f1[0], f1[1])
        m = I(te.matmul(p1[:, o1:o1 + 64], ones1[:, :], ones1[:, 0:64],
                        start=True, stop=True), "SM", 1)
        pool.set_free(i1, "SM", m)

    I(sy.dma_start(out=s1t[:], in_=s1d[:, :]), "SA", 16)
    I(sy.dma_start(out=s2t[:], in_=s2d[:, :]), "SA", 16)
    I(sy.dma_start(out=hs1t[:], in_=hs1d[:, :]), "SA", 16)
    I(sy.dma_start(out=hs2t[:], in_=hs2d[:, :]), "SA", 16)
    I(sy.dma_start(out=bqkvt[:], in_=bqkvd[:, :]), "SA", 16)
    I(sy.dma_start(out=noncet[:], in_=nonced[:, :]), "SA", 16)
    I(g.dma_start(out=eyet[:], in_=eyed[:, :]), "SW", 16)
    I(g.dma_start(out=eye64t[:], in_=eye64d[:, :]), "SW", 16)
    I(g.dma_start(out=MK[:, 0:2048], in_=mnsd[:, :]), "SW", 16)
    I(g.dma_start(out=MK[:, 2048:4096], in_=mstd[:, :]), "SW", 16)
    I(g.dma_start(out=rowbuf[:], in_=s3rowd[:, :]), "SW", 16)

    W(ve, "SW"); W(ve, "SA")
    e_st0 = I(ve.memset(STt[:], 0.0), "SV", 1)
    W(ve, "SV", e_st0)
    ve.memset(ones1[:], 1.0)
    ve.memset(ones64b[64:65, :], 1.0)
    ve.memset(onescol[:], 1.0)
    ve.memset(STt[:, 27:28], float(EPS))
    e_bq8 = I(ve.tensor_scalar_mul(STt[:, 18:26], bqkvt[:, 0:8], 0.125), "SV", 1)
    e_s3 = I(ve.tensor_scalar_mul(rowbuf[:], rowbuf[:], 0.5), "SV", 1)
    W(te, "SV", e_s3)
    for hh in range(2):
        m = I(te.matmul(psD[:, 0:512], ones1[:, :], rowbuf[:, 512 * hh:512 * (hh + 1)],
                        start=True, stop=True), "SM", 1)
        W(ve, "SM", m)
        f = I(ve.tensor_copy(s3bc[:, 512 * hh:512 * (hh + 1)], psD[:, 0:512]),
              "FE", 1)
        W(te, "FE", f)
    consts_sv = cnt["SV"]

    def bcast_row_f32(row_bf, pool, sw_ready=0):
        for hh in range(2):
            i1, p1, o1, f1 = pool.take()
            W(te, f1[0], f1[1])
            if hh == 0 and sw_ready:
                W(te, "SW", sw_ready)
            m = I(te.matmul(p1[:, o1:o1 + 512], ones1[:, :],
                            row_bf[:, 512 * hh:512 * (hh + 1)],
                            start=True, stop=True), "SM", 1)
            W(ve, "SM", m)
            f = I(ve.tensor_copy(vbias[:, 512 * hh:512 * (hh + 1)],
                                 p1[:, o1:o1 + 512]), "FPV", 1)
            pool.set_free(i1, "FPV", f)

    def ln_phase(src_ap, pool=None):
        src_t = src_ap.rearrange("(tc p) c -> tc p c", p=128)
        xfree = {}
        utfree = {}
        W(sy, "SV")
        W(sy, "SE")
        WFP(sy)
        W(sy, "SS")
        W(sy, "SD0"); W(sy, "SD1")
        for tc in range(NTC):
            slot = tc % 2
            if tc - 2 in xfree:
                W(sy, "SE", xfree[tc - 2])
            r = I(sy.dma_start(out=XB[:, 1024 * slot:1024 * (slot + 1)],
                               in_=src_t[tc]), "SX%d" % slot, 16)
            if pool is not None and WARMERS:
                warm(pool)
            xs = XB[:, 1024 * slot:1024 * (slot + 1)]
            sq = AR[:, 1024 * slot:1024 * (slot + 1)]
            col = 9 * slot
            st_ = lambda k: STt[:, col + k:col + k + 1]
            W(ve, "SX%d" % slot, r)
            e_s1 = I(ve.reduce_sum(st_(0), xs, axis=AX, negate=True), "SV", 1)
            W(sc, "SX%d" % slot, r)
            e_sq = I(sc.activation(sq, xs, AF.Square, accum_out=st_(1)), "SE", 1)
            W(ve, "SV", e_s1)
            e1 = I(ve.tensor_scalar_mul(st_(2), st_(0), 1.0 / C), "SV", 1)
            W(ve, "SE", e_sq)
            e2 = I(ve.tensor_scalar_mul(st_(3), st_(1), 1.0 / C), "SV", 1)
            W(ve, "SV", e1)
            e3 = I(ve.tensor_mul(st_(4), st_(2), st_(2)), "SV", 1)
            W(ve, "SV", e3)
            e4 = I(ve.tensor_sub(st_(5), st_(3), st_(4)), "SV", 1)
            W(sc, "SV", e4)
            e5 = I(sc.activation(st_(6), st_(5), AF.Sqrt, bias=STt[:, 27:28]), "SE", 1)
            W(ve, "SE", e5)
            e6 = I(ve.reciprocal(st_(7), st_(6)), "SV", 1)
            W(ve, "SV", e6)
            e7 = I(ve.tensor_mul(st_(8), st_(2), st_(7)), "SV", 1)
            W(sc, "SV", e7)
            if tc - 2 in utfree:
                W(sc, "SM", utfree[tc - 2])
            ut = UT[:, 1024 * slot:1024 * (slot + 1)]
            e8 = I(sc.activation(ut, xs, AF.Identity, bias=st_(8), scale=st_(7)),
                   "SE", 1)
            xfree[tc] = e8
            W(te, "SE", e8)
            for wave in range(2):
                W(te, "FE")
                mlast = None
                for q in range(4):
                    cc = 4 * wave + q
                    mlast = te.transpose(psE[:, 128 * q:128 * (q + 1)],
                                         ut[:, 128 * cc:128 * (cc + 1)], eyet[:])
                m = I(mlast, "SM", 1)
                W(ve, "SM", m)
                outv = BU[:, :].rearrange("p (cc t) -> p cc t", t=T_LOC)[
                    :, 4 * wave:4 * wave + 4, 128 * tc:128 * (tc + 1)]
                ev = I(ve.tensor_copy(outv, psE[:, 0:512].rearrange(
                    "p (cc i) -> p cc i", i=128)), "FE", 1)
            utfree[tc] = m
        return ("FE", cnt["FE"])

    PSLOTS = ([(psA, 512 * k) for k in range(4)] +
              [(psB, 512 * k) for k in range(2)] + [(psD, 0)])

    class SlotPool:
        def __init__(self):
            self.free_at = [("FPV", cnt["FPV"])] * len(PSLOTS)
            self.rr = 0

        def take(self):
            i = self.rr % len(PSLOTS)
            self.rr += 1
            return i, PSLOTS[i][0], PSLOTS[i][1], self.free_at[i]

        def set_free(self, i, sem, v):
            self.free_at[i] = (sem, v)

    def ffn_phase(x_src, x_dst):
        pool = SlotPool()
        u_done = ln_phase(x_src, pool)
        ut_rr = [0]
        ut_freeA = {}
        ut_free = {}

        wready = {}

        def load_w12(j):
            sname = "SW%d" % (j % 2)
            s_ = j % 2
            I(g.dma_start(out=WB[:, 1024 * s_:1024 * (s_ + 1)],
                          in_=w1t[j]), sname, 16)
            r2 = I(g.dma_start(out=WB[:, 1024 * (2 + s_):1024 * (3 + s_)],
                               in_=w2t[j]), sname, 16)
            wready[j] = r2

        W(g, "SM")
        load_w12(0); load_w12(1)

        for j in range(32):
            W(te, "SW%d" % (j % 2), wready[j])
            if j == 0:
                W(te, u_done[0], u_done[1])
            w1s = WB[:, 1024 * (j % 2):1024 * (j % 2 + 1)]
            w2s = WB[:, 1024 * (2 + j % 2):1024 * (3 + j % 2)]
            for tq in range(4):
                i1, p1, o1, f1 = pool.take()
                i2, p2, o2, f2 = pool.take()
                W(te, f1[0], f1[1]); W(te, f2[0], f2[1])
                for ck in range(8):
                    te.matmul(p1[:, o1:o1 + 512], w1s[:, 128 * ck:128 * (ck + 1)],
                              BU[:, 2048 * ck + 512 * tq:2048 * ck + 512 * (tq + 1)],
                              start=(ck == 0), stop=(ck == 7))
                mm = None
                for ck in range(8):
                    mm = te.matmul(p2[:, o2:o2 + 512],
                                   w2s[:, 128 * ck:128 * (ck + 1)],
                                   BU[:, 2048 * ck + 512 * tq:2048 * ck + 512 * (tq + 1)],
                                   start=(ck == 0), stop=(ck == 7))
                mgrp = I(mm, "SM", 1)
                u1 = 512 * (ut_rr[0] % 4); ut_rr[0] += 1
                u2 = 512 * (ut_rr[0] % 4); ut_rr[0] += 1
                W(sc, "SM", mgrp)
                if u1 in ut_freeA:
                    W(sc, "FPV", ut_freeA[u1])
                if u2 in ut_freeA:
                    W(sc, "FPV", ut_freeA[u2])
                ea = I(sc.activation(UT[:, u1:u1 + 512], p1[:, o1:o1 + 512],
                                     AF.Silu, scale=s1t[:, j:j + 1]), "SE", 1)
                eb = I(sc.activation(UT[:, u2:u2 + 512], p2[:, o2:o2 + 512],
                                     AF.Copy, scale=s2t[:, j:j + 1]), "SE", 1)
                W(ve, "SE", eb)
                f = I(ve.tensor_mul(AR[:, 2048 * j + 512 * tq:2048 * j + 512 * (tq + 1)],
                                    UT[:, u1:u1 + 512], UT[:, u2:u2 + 512]), "FPV", 2)
                ut_freeA[u1] = f; ut_freeA[u2] = f
                pool.set_free(i1, "FPV", f); pool.set_free(i2, "FPV", f)
            if j + 2 < 32:
                W(g, "SM", mgrp)
                load_w12(j + 2)

        def big_mid(win, wsc_t, src_is_arena, src_d, dst_d):
            wr = {}

            def load_big(j):
                s_ = j % 2
                wr[j] = I(g.dma_start(out=WB[:, 4096 * s_:4096 * (s_ + 1)],
                                      in_=win[j]), "SB%d" % (j % 2), 16)

            W(g, "SM")
            load_big(0); load_big(1)
            if not src_is_arena:
                lded = 0
                lded_g = 0
                for jj in range(32):
                    for t in range(4):
                        dstv = AR[:, 2048 * jj + 512 * t:2048 * jj + 512 * (t + 1)]
                        if (4 * jj + t) % 2 == 0:
                            lded = I(sy.dma_start(out=dstv, in_=src_d[jj, t]),
                                     "SH", 16)
                        else:
                            lded_g = I(g.dma_start(out=dstv, in_=src_d[jj, t]),
                                       "SHG", 16)
            for j2 in range(32):
                W(te, "SB%d" % (j2 % 2), wr[j2])
                if j2 == 0:
                    if src_is_arena:
                        WFP(te)
                    else:
                        W(te, "SH", lded)
                        W(te, "SHG", lded_g)
                ws = WB[:, 4096 * (j2 % 2):4096 * (j2 % 2 + 1)]
                for tq in range(4):
                    i1, p1, o1, f1 = pool.take()
                    W(te, f1[0], f1[1])
                    mm = None
                    for ck in range(32):
                        mm = te.matmul(
                            p1[:, o1:o1 + 512], ws[:, 128 * ck:128 * (ck + 1)],
                            AR[:, 2048 * ck + 512 * tq:2048 * ck + 512 * (tq + 1)],
                            start=(ck == 0), stop=(ck == 31))
                    mgrp = I(mm, "SM", 1)
                    uslot = ut_rr[0] % 4
                    uo = 512 * uslot; ut_rr[0] += 1
                    W(sc, "SM", mgrp)
                    if uslot in ut_free:
                        W(sc, "SU%d" % uslot, ut_free[uslot])
                    e = I(sc.activation(UT[:, uo:uo + 512], p1[:, o1:o1 + 512],
                                        AF.Gelu, scale=wsc_t[:, j2:j2 + 1]), "FPS", 1)
                    pool.set_free(i1, "FPS", e)
                    W(sy, "FPS", e)
                    ut_free[uslot] = I(sy.dma_start(
                        out=dst_d[j2, tq],
                        in_=UT[:, uo:uo + 512]), "SU%d" % uslot, 16)
                if j2 + 2 < 32:
                    W(g, "SM", mgrp)
                    load_big(j2 + 2)
            return None

        big_mid(h1t, hs1t, True, None, h2_d)
        for u_ in range(4):
            W(sy, "SU%d" % u_)
        big_mid(h2t, hs2t, False, h2_d, h3_d)
        for u_ in range(4):
            W(sy, "SU%d" % u_)

        lded = 0
        lded_g = 0
        W(g, "SM")
        for jj in range(32):
            for t in range(4):
                dstv = AR[:, 2048 * jj + 512 * t:2048 * jj + 512 * (t + 1)]
                if (4 * jj + t) % 2 == 0:
                    lded = I(sy.dma_start(out=dstv, in_=h3_d[jj, t]), "SH", 16)
                else:
                    lded_g = I(g.dma_start(out=dstv, in_=h3_d[jj, t]),
                               "SHG", 16)
        W(sy, "FA")
        x_src_t = x_src.rearrange("(tc p) c -> tc p c", p=128)
        x_dst_t = x_dst.rearrange("(tc p) c -> tc p c", p=128)
        w3v = w3t.rearrange("(k p) m -> k p m", p=128)
        xload = {}
        xst = {}
        supers = [list(range(s, min(s + 3, NTC))) for s in range(0, NTC, 3)]
        first = True
        for sup in supers:
            wr = {}

            def load_w3(k):
                wr[k] = I(g.dma_start(out=WB[:, 1024 * (k % 2):1024 * (k % 2 + 1)],
                                      in_=w3v[k]), "SW%d" % (k % 2), 16)

            W(g, "SM")
            load_w3(0); load_w3(1)
            def load_x(ti):
                prev = ti - 2
                if prev in xst:
                    W(sy, "SD%d" % (prev % 2), xst[prev])
                xload[ti] = I(sy.dma_start(
                    out=XB[:, 1024 * (ti % 2):1024 * (ti % 2 + 1)],
                    in_=x_src_t[ti]), "SX%d" % (ti % 2), 16)

            for ti in sup[:2]:
                load_x(ti)
            ts = {}
            for ti in sup:
                a = pool.take(); b = pool.take()
                W(te, a[3][0], a[3][1]); W(te, b[3][0], b[3][1])
                ts[ti] = (a, b)
            if first:
                W(te, "SH", lded)
                W(te, "SHG", lded_g)
                first = False
            last_mm = None
            for k in range(32):
                W(ve, "SW%d" % (k % 2), wr[k])
                wvs = WB[:, 1024 * (k % 2):1024 * (k % 2 + 1)]
                esc = I(ve.tensor_mul(wvs, wvs, s3bc[:]), "SV", 1)
                W(te, "SV", esc)
                for ti in sup:
                    (ia, pa, oa, _), (ib, pb, ob, _) = ts[ti]
                    te.matmul(pa[:, oa:oa + 512],
                              AR[:, 2048 * k + 128 * ti:2048 * k + 128 * (ti + 1)],
                              wvs[:, 0:512], start=(k == 0), stop=(k == 31))
                    last_mm = te.matmul(
                        pb[:, ob:ob + 512],
                        AR[:, 2048 * k + 128 * ti:2048 * k + 128 * (ti + 1)],
                        wvs[:, 512:1024], start=(k == 0), stop=(k == 31))
                km = I(last_mm, "SM", 1)
                if k + 2 < 32:
                    W(g, "SM", km)
                    load_w3(k + 2)
            mgrp = km
            for idx, ti in enumerate(sup):
                if idx == 2:
                    load_x(ti)
                (ia, pa, oa, _), (ib, pb, ob, _) = ts[ti]
                W(ve, "SM", mgrp)
                W(ve, "SX%d" % (ti % 2), xload[ti])
                xv = XB[:, 1024 * (ti % 2):1024 * (ti % 2 + 1)]
                ve.tensor_add(xv[:, 0:512], pa[:, oa:oa + 512], xv[:, 0:512])
                f = I(ve.tensor_add(xv[:, 512:1024], pb[:, ob:ob + 512],
                                    xv[:, 512:1024]), "FPV", 2)
                pool.set_free(ia, "FPV", f); pool.set_free(ib, "FPV", f)
                W(sy, "FPV", f)
                xst[ti] = I(sy.dma_start(out=x_dst_t[ti], in_=xv),
                            "SD%d" % (ti % 2), 16)
        return None

    def finish(engines_extra=()):
        W(sy, "SS")
        W(sy, "SD0"); W(sy, "SD1")
        W(g, "SS")
        W(g, "SW")
        W(g, "SHG")
        W(g, "SD0"); W(g, "SD1")
        blk_guard.__exit__(None, None, None)
        return nc, ctx

    def copy_dram_to_out(src, extra_done):
        st = src.rearrange("(tc p) c -> tc p c", p=128)
        yt = yout.rearrange("(tc p) c -> tc p c", p=128)
        W(sy, "SS", extra_done)
        W(sy, "SE"); W(sy, "SV"); WFP(sy)
        for tc in range(NTC):
            r = I(sy.dma_start(out=XB[:, 0:1024], in_=st[tc]), "SA", 16)
            W(sy, "SA", r)
            s = I(sy.dma_start(out=yt[tc], in_=XB[:, 0:1024]), "SS", 16)
            W(sy, "SS", s)

    if DEBUG_STAGE == 1:
        u_done = ln_phase(x0)
        W(g, "SV", u_done)
        I(g.dma_start(out=dbg["u1"][:, :], in_=BU[:, :]), "SW", 16)
        W(sy, "SV", u_done)
        copy_dram_to_out(x0, 0)
        return finish()

    dst1 = yout if DEBUG_STAGE == 2 else x1_d
    ffn_phase(x0, dst1)
    if DEBUG_STAGE == 2:
        return finish()

    W(sy, "SD0"); W(sy, "SD1")
    pool = SlotPool()
    u2_done = ln_phase(x1_d, pool)

    wr = {}

    def load_qk(oc):
        s_ = oc % 2
        wr[oc] = I(g.dma_start(out=WB[:, 1024 * s_:1024 * (s_ + 1)],
                               in_=wqkv[oc]), "SW%d" % (oc % 2), 16)

    W(g, "SM")
    load_qk(0); load_qk(1)
    for oc in range(16):
        W(te, "SW%d" % (oc % 2), wr[oc])
        if oc == 0:
            W(te, u2_done[0], u2_done[1])
        ws = WB[:, 1024 * (oc % 2):1024 * (oc % 2 + 1)]
        for tq in range(4):
            i1, p1, o1, f1 = pool.take()
            W(te, f1[0], f1[1])
            mm = None
            for ck in range(8):
                mm = te.matmul(
                    p1[:, o1:o1 + 512], ws[:, 128 * ck:128 * (ck + 1)],
                    BU[:, 2048 * ck + 512 * tq:2048 * ck + 512 * (tq + 1)],
                    start=(ck == 0), stop=(ck == 7))
            mgrp = I(mm, "SM", 1)
            W(sc, "SM", mgrp)
            if oc < 8:
                bias_ap = STt[:, 18 + oc:19 + oc]
                scl = 0.125
            else:
                bias_ap = bqkvt[:, oc:oc + 1]
                scl = 1.0
            e = I(sc.activation(
                AR[:, A_XQK + 2048 * oc + 512 * tq:A_XQK + 2048 * oc + 512 * (tq + 1)],
                p1[:, o1:o1 + 512], AF.Identity, bias=bias_ap, scale=scl), "FPS", 1)
            pool.set_free(i1, "FPS", e)
        if oc + 2 < 16:
            W(g, "SM", mgrp)
            load_qk(oc + 2)

    W(g, "FE")
    rb_v = I(g.dma_start(out=rowbuf[:], in_=bqkvrowd[:, 2048:3072]), "SW", 16)
    bcast_row_f32(rowbuf, pool, sw_ready=rb_v)
    W(g, "FPV")
    W(g, "SM")
    vrdy = 0
    for cc in range(8):
        vrdy = I(g.dma_start(out=WB[:, 1024 * cc:1024 * (cc + 1)],
                             in_=wqkvv[cc]), "SW", 16)
    W(ve, "SM")
    for tc in range(NTC):
        base = A_XV + 1040 * tc
        vv = AR[:, base:base + 1040].rearrange("p (gg e) -> p gg e", e=65)
        I(ve.memset(vv[:, :, 64:65], 1.0), "SV", 1)
    vb_sv = cnt["FPV"]
    for ti in range(NTC):
        i1, p1, o1, f1 = pool.take()
        i2, p2, o2, f2 = pool.take()
        W(te, f1[0], f1[1]); W(te, f2[0], f2[1])
        if ti == 0:
            W(te, "SW", vrdy)
        mm = None
        for cc in range(8):
            te.matmul(p1[:, o1:o1 + 512],
                      BU[:, 2048 * cc + 128 * ti:2048 * cc + 128 * (ti + 1)],
                      WB[:, 1024 * cc:1024 * cc + 512],
                      start=(cc == 0), stop=(cc == 7))
            mm = te.matmul(p2[:, o2:o2 + 512],
                           BU[:, 2048 * cc + 128 * ti:2048 * cc + 128 * (ti + 1)],
                           WB[:, 1024 * cc + 512:1024 * (cc + 1)],
                           start=(cc == 0), stop=(cc == 7))
        mgrp = I(mm, "SM", 1)
        W(ve, "SM", mgrp)
        W(ve, "FPV", vb_sv)
        base = A_XV + 1040 * ti
        ov = AR[:, base:base + 1040].rearrange("p (gg e) -> p gg e", e=65)
        ve.tensor_add(ov[:, 0:8, 0:64],
                      p1[:, o1:o1 + 512].rearrange("p (gg e) -> p gg e", e=64),
                      vbias[:, 0:512].rearrange("p (gg e) -> p gg e", e=64))
        f = I(ve.tensor_add(ov[:, 8:16, 0:64],
                            p2[:, o2:o2 + 512].rearrange("p (gg e) -> p gg e", e=64),
                            vbias[:, 512:1024].rearrange("p (gg e) -> p gg e", e=64)),
              "FPV", 2)
        pool.set_free(i1, "FPV", f); pool.set_free(i2, "FPV", f)

    sq_used = {}
    for cc in range(16):
        sl = cc % 2
        sq = AR[:, A_Y + 2048 * sl:A_Y + 2048 * (sl + 1)]
        if sl in sq_used:
            W(sc, "SM", sq_used[sl])
        e = I(sc.activation(sq, AR[:, A_XQK + 2048 * cc:A_XQK + 2048 * (cc + 1)],
                            AF.Square), "SE", 1)
        W(te, "SE", e)
        takes = [pool.take() for _ in range(4)]
        for tk_ in takes:
            W(te, tk_[3][0], tk_[3][1])
        mm = None
        for q in range(4):
            i_, p_, o_, _ = takes[q]
            mm = te.matmul(p_[0:1, o_:o_ + 512], onescol[:, :],
                           sq[:, 512 * q:512 * (q + 1)],
                           start=True, stop=True)
            I(mm, "SM", 1)
        mgrp = cnt["SM"]
        sq_used[sl] = mgrp
        W(ve, "SM", mgrp)
        W(ve, "SV")
        for q in range(4):
            i_, p_, o_, _ = takes[q]
            f = I(ve.reduce_max(STt[0:1, 36 + q:37 + q], p_[0:1, o_:o_ + 512],
                                axis=AX), "FPV", 1)
            pool.set_free(i_, "FPV", f)
        W(ve, "FPV")
        I(ve.reduce_max(STt[0:1, 40 + cc:41 + cc], STt[0:1, 36:40], axis=AX),
          "SV", 1)
    W(ve, "FPV")
    W(ve, "SV")
    ej = I(ve.reduce_max(STt[0:1, 56:57], STt[0:1, 40:48], axis=AX), "SV", 1)
    W(ve, "SV", ej)
    ek = I(ve.reduce_max(STt[0:1, 57:58], STt[0:1, 48:56], axis=AX), "SV", 1)
    W(sc, "SV", ek)
    es1 = I(sc.activation(STt[0:1, 58:59], STt[0:1, 56:57], AF.Sqrt), "SE", 1)
    es2 = I(sc.activation(STt[0:1, 59:60], STt[0:1, 57:58], AF.Sqrt), "SE", 1)
    W(ve, "SE", es2)
    em = I(ve.tensor_mul(STt[0:1, 60:61], STt[0:1, 58:59], STt[0:1, 59:60]),
           "SV", 1)
    W(ve, "SV", em)
    em2 = I(ve.tensor_scalar_mul(STt[0:1, 61:62], STt[0:1, 60:61], -1.0), "SV", 1)
    W(ve, "SV", em2)
    eb = I(ve.tensor_copy(UT[0:1, 1:2], STt[0:1, 61:62]), "SV", 1)
    W(te, "SV", eb)
    i1, p1, o1, f1 = pool.take()
    W(te, f1[0], f1[1])
    mb = I(te.matmul(p1[:, o1:o1 + 1], ones1[:, :], UT[0:1, 1:2], start=True,
                     stop=True), "SM", 1)
    W(sc, "SM", mb)
    e_nm = I(sc.activation(STt[:, 26:27], p1[:, o1:o1 + 1], AF.Copy), "FPS", 1)
    pool.set_free(i1, "FPS", e_nm)
    negmh = STt[:, 26:27]

    def ydst_even(ytile, g_q):
        return ytile[0:64, :].rearrange("p (a sx) -> p a sx", sx=16)[:, :, g_q]

    def ydst_odd(ytile, g_q):
        return ytile[64:128, :].rearrange("p (a sx) -> p a sx", sx=16)[:, :, g_q]

    SGRP = [(psA, 0), (psA, 1024), (psB, 0)]
    rot = {"c": 0}
    grp_free = [("SE", 0)] * 3
    pipe = {"mv": {}, "masks": {}, "eq": {}, "pend": None}

    def ptb(g):
        return PT if g % 2 == 0 else BU

    def emit_rebase(h_loc, g_q):
        t0 = 256 * h_loc
        qcc = g_q // 2
        qof = 64 * (g_q % 2)
        qof2 = 64 - qof
        rhs_q = AR[qof:qof + 64,
                   A_XQK + 2048 * qcc + t0:A_XQK + 2048 * qcc + t0 + 256]
        W(te, "SE", cnt["SE"])
        WFP(te)
        mq = I(te.matmul(psD[qof2:qof2 + 64, 256:512],
                         eye64t[qof:qof + 64, :], rhs_q,
                         start=True, stop=True), "SM", 1)
        W(ve, "SM", mq)
        eq = I(ve.tensor_copy(qstage[qof2:qof2 + 64, :],
                              psD[qof2:qof2 + 64, 256:512]), "SV", 1)
        pipe["eq"][(h_loc, g_q)] = eq
        return rhs_q, qstage[qof2:qof2 + 64, :], qof

    def emit_s(h_loc, g_q, rhs_q, rhs_q2, qof):
        t0 = 256 * h_loc
        PTg = ptb(g_q)
        W(te, "SV", pipe["eq"][(h_loc, g_q)])
        dbg3 = (DEBUG_STAGE == 3)
        for fill in range(0 if not (dbg3 and ATTN_PART < 2) else 4, 4):
            gi = rot["c"] % 3
            rot["c"] += 1
            gten, gbase = SGRP[gi]
            W(te, grp_free[gi][0], grp_free[gi][1])
            lm = None
            for q in range(4):
                g_k = 4 * fill + 2 * (q // 2) + (q % 2)
                co = gbase + 512 * (q % 2) + 256 * (q // 2)
                kcc = 8 + g_k // 2
                kof = 64 * (g_k % 2)
                lhs_k = AR[kof:kof + 64,
                           A_XQK + 2048 * kcc + t0:A_XQK + 2048 * kcc + t0 + 128]
                rq = rhs_q if kof == qof else rhs_q2
                lm = te.matmul(gten[:, co:co + 256],
                               lhs_k, rq, start=True, stop=True)
            m = I(lm, "SM", 1)
            W(sc, "SM", m)
            dst5 = PTg[:, PT_R1:PT_R1 + 4096].rearrange(
                "p (hh gkh gkl i) -> p gkh gkl hh i", hh=2, gkh=8, gkl=2, i=128)
            e = None
            for bk in range(2):
                dst = dst5[:, 2 * fill:2 * fill + 2, bk, :, :]
                srcv = gten[:, gbase + 512 * bk:gbase + 512 * (bk + 1)
                            ].rearrange("p (q2 hh i) -> p q2 hh i",
                                        q2=2, hh=2, i=128)
                e = I(sc.activation(dst, srcv, AF.Exp, bias=negmh), "SE", 1)
            grp_free[gi] = ("SE", e)
        for fill in range(0 if not (dbg3 and ATTN_PART < 3) else 4, 4):
            gi = rot["c"] % 3
            rot["c"] += 1
            gten, gbase = SGRP[gi]
            W(te, grp_free[gi][0], grp_free[gi][1])
            lm = None
            for q in range(4):
                g_k = 4 * fill + 2 * (q // 2) + (q % 2)
                co = gbase + 512 * (q % 2) + 256 * (q // 2)
                kcc = 8 + g_k // 2
                kof = 64 * (g_k % 2)
                lhs_k = AR[kof:kof + 64,
                           A_XQK + 2048 * kcc + t0 + 128:
                           A_XQK + 2048 * kcc + t0 + 256]
                rq = rhs_q if kof == qof else rhs_q2
                lm = te.matmul(gten[:, co:co + 128],
                               lhs_k, rq[:, 128:256], start=True, stop=True)
            m = I(lm, "SM", 1)
            W(sc, "SM", m)
            d3 = PTg[:, PT_R3 + 512 * fill:PT_R3 + 512 * (fill + 1)].rearrange(
                "p (q2 gkl i) -> p q2 gkl i", q2=2, gkl=2, i=128)
            e = None
            for bk in range(2):
                dst = d3[:, :, bk, :]
                srcv = gten[:, gbase + 512 * bk:gbase + 512 * (bk + 1)
                            ].rearrange("p (q2 i) -> p q2 i",
                                        q2=2, i=256)[:, :, 0:128]
                e = I(sc.activation(dst, srcv, AF.Exp, bias=negmh), "SE", 1)
            grp_free[gi] = ("SE", e)
        if dbg3 and ATTN_PART < 4:
            pipe["masks"][(h_loc, g_q)] = cnt["SV"]
            return
        W(ve, "SE")
        nsw = 128 * (g_q + 1)
        I(ve.tensor_mul(PTg[:, PT_R1:PT_R1 + nsw], PTg[:, PT_R1:PT_R1 + nsw],
                        MK[:, 0:nsw]), "SV", 1)
        if nsw < 2048:
            I(ve.tensor_mul(PTg[:, PT_R1 + nsw:PT_R1 + 2048],
                            PTg[:, PT_R1 + nsw:PT_R1 + 2048],
                            MK[:, 2048:4096 - nsw]), "SV", 1)
        I(ve.tensor_mul(PTg[:, PT_R3:PT_R3 + nsw], PTg[:, PT_R3:PT_R3 + nsw],
                        MK[:, 0:nsw]), "SV", 1)
        if nsw < 2048:
            I(ve.tensor_mul(PTg[:, PT_R3 + nsw:PT_R3 + 2048],
                            PTg[:, PT_R3 + nsw:PT_R3 + 2048],
                            MK[:, 2048:4096 - nsw]), "SV", 1)
        pipe["masks"][(h_loc, g_q)] = cnt["SV"]

    def emit_av_norm(h_loc, g_q, eq_succ=None):
        t0 = 256 * h_loc
        PTg = ptb(g_q)
        W(te, "SV", pipe["masks"][(h_loc, g_q)])
        if eq_succ is not None:
            W(te, "SV", eq_succ)
        WFP(te)
        lm = None
        tc0 = 2 * h_loc
        tc1 = 2 * h_loc + 1
        for g_k in range(16):
            lhsv = AR[:, A_XV + 1040 * tc0 + 65 * g_k:
                      A_XV + 1040 * tc0 + 65 * (g_k + 1)]
            lm = te.matmul(psD[0:65, 0:128], lhsv,
                           PTg[:, PT_R1 + 128 * g_k:PT_R1 + 128 * (g_k + 1)],
                           start=(g_k == 0), stop=(g_k == 15))
        for g_k in range(16):
            lhsv = AR[:, A_XV + 1040 * tc0 + 65 * g_k:
                      A_XV + 1040 * tc0 + 65 * (g_k + 1)]
            te.matmul(psD[0:65, 128:256], lhsv,
                      PTg[:, PT_R2 + 128 * g_k:PT_R2 + 128 * (g_k + 1)],
                      start=(g_k == 0), stop=False)
        for g_k in range(16):
            lhsv = AR[:, A_XV + 1040 * tc1 + 65 * g_k:
                      A_XV + 1040 * tc1 + 65 * (g_k + 1)]
            lm = te.matmul(psD[0:65, 128:256], lhsv,
                           PTg[:, PT_R3 + 128 * g_k:PT_R3 + 128 * (g_k + 1)],
                           start=False, stop=(g_k == 15))
        mv = I(lm, "SM", 1)
        pipe["mv"][(h_loc, g_q)] = mv
        W(ve, "SM", mv)
        er = I(ve.reciprocal(ripr2[64:65, 0:128], psD[64:65, 0:128]), "SV", 1)
        W(ve, "SV", er)
        er = I(ve.reciprocal(ripr2[64:65, 128:256], psD[64:65, 128:256]), "SV", 1)
        W(ve, "SV", er)
        W(te, "SV", er)
        mb2 = I(te.matmul(psD[0:64, 256:512], ones64b[64:65, :],
                          ripr2[64:65, :], start=True, stop=True), "SM", 1)
        W(sc, "SM", mb2)
        ec = I(sc.activation(ripb[:, :], psD[0:64, 256:512], AF.Copy), "SE", 1)
        W(ve, "SE", ec)
        ytile = AR[:, A_Y + 4096 * (h_loc // 2):A_Y + 4096 * (h_loc // 2 + 1)]
        if h_loc % 2 == 0:
            yd = ydst_even(ytile, g_q)
            I(ve.tensor_mul(yd[:, 0:128], psD[0:64, 0:128],
                            ripb[:, 0:128]), "FPV", 1)
            I(ve.tensor_mul(yd[:, 128:256], psD[0:64, 128:256],
                            ripb[:, 128:256]), "FPV", 1)
        else:
            I(ve.tensor_mul(ytmp[0:64, 0:128], psD[0:64, 0:128],
                            ripb[:, 0:128]), "SV", 1)
            en = I(ve.tensor_mul(ytmp[0:64, 128:256], psD[0:64, 128:256],
                                 ripb[:, 128:256]), "SV", 1)
            W(te, "SV", en)
            mc = I(te.matmul(psD[64:128, 256:512], eyet[0:64, 0:64],
                             ytmp[0:64, :], start=True, stop=True), "SM", 1)
            W(ve, "SM", mc)
            I(ve.tensor_copy(ydst_odd(ytile, g_q), psD[64:128, 256:512]),
              "FPV", 1)

    def head_attn(h_loc):
        ngq = ATTN_GQ if DEBUG_STAGE == 3 else 16
        for g_q in range(ngq):
            rq, rq2, qof = emit_rebase(h_loc, g_q)
            emit_s(h_loc, g_q, rq, rq2, qof)
            if DEBUG_STAGE == 3 and ATTN_PART < 5:
                continue
            if pipe["pend"] is not None:
                emit_av_norm(*pipe["pend"],
                             eq_succ=pipe["eq"][(h_loc, g_q)])
            pipe["pend"] = (h_loc, g_q)

    NH = ATTN_HEADS if DEBUG_STAGE == 3 else 8
    W(te, "FPV"); W(te, "FPS")
    for h_loc in range(NH):
        head_attn(h_loc)
    if pipe["pend"] is not None and not (DEBUG_STAGE == 3 and ATTN_PART < 5):
        emit_av_norm(*pipe["pend"])
        pipe["pend"] = None
    y_done = cnt["FPV"]

    W(g, "SM")
    wprdy = 0
    for cc in range(8):
        wprdy = I(g.dma_start(out=WB[:, 1024 * cc:1024 * (cc + 1)],
                              in_=wproj[cc]), "SW", 16)

    if DEBUG_STAGE == 3:
        W(g, "FPV", y_done)
        W(g, "SV")
        I(g.dma_start(out=dbg["y"][:, :], in_=AR[:, A_Y:A_Y + 16384]), "SW", 16)
        I(g.dma_start(out=dbg["xp"][:, :], in_=AR[:, A_XQK:A_XQK + 8192]),
          "SW", 16)
        copy_dram_to_out(x1_d, 0)
        return finish()

    flag_r = g.alloc_register("flag_r"); flag_r2 = g.alloc_register("flag_r2")
    n0_r = g.alloc_register("n0_r"); n1_r = g.alloc_register("n1_r")
    cnt_r = g.alloc_register("cnt_r"); cond_r = g.alloc_register("cond_r")
    t_r = g.alloc_register("t_r")
    ss0 = cnt["SS"]
    sw0 = cnt["SW"]
    W(g, "SA")
    for arm in nc.switch_core_id(engines=[sy, g], n=128):
        p = int(arm) % 2
        q = p ^ 1
        pcols = 2048 * (1 - p)
        sy.wait_ge(sems["FPV"], y_done)
        for j in range(4):
            sy.dma_start(out=sh_y[p, j, :, :],
                         in_=AR[:, A_Y + 4096 * j + pcols:
                                A_Y + 4096 * j + pcols + 2048]
                         ).then_inc(sems["SS"], 16)
        sy.wait_ge(sems["SS"], ss0 + 64)
        sy.dma_start(out=sh_flag[p, :, :], in_=noncet[:]).then_inc(sems["SS"], 16)
        g.wait_ge(sems["SW"], sw0)
        g.reg_load(n0_r, noncet[0:1, 0:1])
        g.reg_load(n1_r, noncet[0:1, 1:2])
        g.reg_mov(cnt_r, 0)
        g.reg_mov(cond_r, 1)
        with g.While(cond_r):
            g.dma_start(out=flagt[:], in_=sh_flag[q, :, :]).then_inc(sems["PS"], 16)
            g.reg_add(cnt_r, cnt_r, 16)
            g.wait_ge(sems["PS"], cnt_r)
            g.reg_load(flag_r, flagt[0:1, 0:1])
            g.reg_load(flag_r2, flagt[0:1, 1:2])
            g.reg_alu(cond_r, flag_r, n0_r, mybir.AluOpType.not_equal)
            g.reg_alu(t_r, flag_r2, n1_r, mybir.AluOpType.not_equal)
            g.reg_alu(cond_r, cond_r, t_r, mybir.AluOpType.logical_or)
        g.nop().then_inc(sems["SG"], 1)
        sy.wait_ge(sems["SG"], 1)
        for j in range(4):
            sy.dma_start(out=AR[:, A_Y + 4096 * j + pcols:
                                A_Y + 4096 * j + pcols + 2048],
                         in_=sh_y[q, j, :, :]).then_inc(sems["SS"], 16)
    cnt["SG"] += 1
    cnt["SS"] += 64 + 16 + 64
    exch_done = cnt["SS"]

    rb_p = I(g.dma_start(out=rowbuf[:], in_=bprojrowd[:, :]), "SW", 16)
    W(ve, "SV")
    bcast_row_f32(rowbuf, pool, sw_ready=rb_p)
    bp_sv = cnt["FPV"]
    x1t = x1_d.rearrange("(tc p) c -> tc p c", p=128)
    x2t = x2_d.rearrange("(tc p) c -> tc p c", p=128)
    base_c = {k: cnt[k] for k in ("SM", "SA", "FPV", "SS", "SX0", "SX1",
                                  "SD0", "SD1")}
    for arm in nc.switch_core_id(engines=[te, sc, ve, sy], n=128):
        half = int(arm) % 2
        cc_ = dict(base_c)
        rr = 0
        sy.wait_ge(sems["FA"], cnt["FA"])
        te.wait_ge(sems["SS"], exch_done)
        te.wait_ge(sems["SW"], cnt["SW"])
        te.wait_ge(sems["FPV"], bp_sv)
        xst2 = {}
        for ti in range(NTC):
            sa = 512 * (rr % 4); sbo = 512 * ((rr + 1) % 4)
            rr += 2
            te.wait_ge(sems["FPV"], cc_["FPV"])
            lm = None
            for cc in range(8):
                j = cc % 4
                cols = 2048 * half if (cc // 4) == half else 2048 * (1 - half)
                yv = AR[:, A_Y + 4096 * j + cols + 128 * ti:
                        A_Y + 4096 * j + cols + 128 * (ti + 1)]
                te.matmul(psA[:, sa:sa + 512], yv, WB[:, 1024 * cc:1024 * cc + 512],
                          start=(cc == 0), stop=(cc == 7))
                lm = te.matmul(psA[:, sbo:sbo + 512], yv,
                               WB[:, 1024 * cc + 512:1024 * (cc + 1)],
                               start=(cc == 0), stop=(cc == 7))
            cc_["SM"] += 1
            lm.then_inc(sems["SM"], 1)
            if ti - 2 in xst2:
                sy.wait_ge(sems["SD%d" % (ti % 2)], xst2[ti - 2])
            sxn = "SX%d" % (ti % 2)
            sy.dma_start(out=XB[:, 1024 * (ti % 2):1024 * (ti % 2 + 1)],
                         in_=x1t[ti]).then_inc(sems[sxn], 16)
            cc_[sxn] += 16
            ve.wait_ge(sems["SM"], cc_["SM"])
            ve.wait_ge(sems["SX%d" % (ti % 2)], cc_["SX%d" % (ti % 2)])
            xv = XB[:, 1024 * (ti % 2):1024 * (ti % 2 + 1)]
            ve.tensor_add(xv[:, 0:512], psA[:, sa:sa + 512], xv[:, 0:512])
            ve.tensor_add(xv[:, 512:1024], psA[:, sbo:sbo + 512],
                          xv[:, 512:1024])
            ve.tensor_add(xv[:, 0:512], xv[:, 0:512], vbias[:, 0:512])
            ve.tensor_add(xv[:, 512:1024], xv[:, 512:1024],
                          vbias[:, 512:1024]).then_inc(sems["FPV"], 2)
            cc_["FPV"] += 2
            sy.wait_ge(sems["FPV"], cc_["FPV"])
            sdn = "SD%d" % (ti % 2)
            xst2[ti] = cc_[sdn] + 16
            sy.dma_start(out=x2t[ti], in_=xv).then_inc(sems[sdn], 16)
            cc_[sdn] += 16
    for k in base_c:
        cnt[k] = cc_[k]

    if DEBUG_STAGE == 4:
        W(sy, "SD0"); W(sy, "SD1")
        copy_dram_to_out(x2_d, 0)
        return finish()

    ffn_phase(x2_d, yout)
    return finish()



_CACHE = {}


def kernel(L0_attn_x, L0_attn_Wqkv, L0_attn_bqkv, L0_attn_Wproj, L0_attn_bproj,
           L0_ffn_w1_w, L0_ffn_w1_s, L0_ffn_w2_w, L0_ffn_w2_s,
           L0_ffn_h1_w, L0_ffn_h1_s, L0_ffn_h2_w, L0_ffn_h2_s,
           L0_ffn_w3_w, L0_ffn_w3_s):
    x = np.ascontiguousarray(np.asarray(L0_attn_x, dtype=np.float32))
    B, T, Cm = x.shape
    asc = lambda a: np.ascontiguousarray(np.asarray(a, dtype=np.float32))

    def tile_in_out(wT, nj):
        cin = wT.shape[0]
        return np.ascontiguousarray(
            wT.reshape(cin // 128, 128, nj, 128).transpose(2, 1, 0, 3)
            .reshape(nj, 128, cin))

    w1tv = tile_in_out(asc(np.asarray(L0_ffn_w1_w).T), 32)
    w2tv = tile_in_out(asc(np.asarray(L0_ffn_w2_w).T), 32)
    h1tv = tile_in_out(asc(np.asarray(L0_ffn_h1_w).T), 32)
    h2tv = tile_in_out(asc(np.asarray(L0_ffn_h2_w).T), 32)
    w3tv = asc(np.asarray(L0_ffn_w3_w).T)
    wq_all = asc(L0_attn_Wqkv)
    wqkvv_qk = tile_in_out(wq_all[:, 0:2048], 16)
    wqkvv_v = np.ascontiguousarray(
        wq_all[:, 2048:3072].reshape(8, 128, 1024))
    wprojv = np.ascontiguousarray(asc(L0_attn_Wproj).reshape(8, 128, 1024))
    s1v = asc(np.asarray(L0_ffn_w1_s).reshape(32, 128).T)
    s2v = asc(np.asarray(L0_ffn_w2_s).reshape(32, 128).T)
    hs1v = asc(np.asarray(L0_ffn_h1_s).reshape(32, 128).T)
    hs2v = asc(np.asarray(L0_ffn_h2_s).reshape(32, 128).T)
    s3rowv = asc(L0_ffn_w3_s).reshape(1, C)
    bqkvv_ = asc(np.asarray(L0_attn_bqkv).reshape(24, 128).T)
    bqkvrow = asc(L0_attn_bqkv).reshape(1, 3 * C)
    bprojrow = asc(L0_attn_bproj).reshape(1, C)
    p_ = np.arange(128)[:, None]
    cq = np.arange(2048)[None, :] % 128
    mns = (p_ <= cq).astype(np.float32)
    mst = (p_ < cq).astype(np.float32)
    eye = np.eye(128, dtype=np.float32)
    eye64 = np.zeros((128, 64), dtype=np.float32)
    eye64[np.arange(128), np.arange(128) % 64] = 1.0
    nonce = np.random.default_rng().integers(1, 2**31 - 1, size=(1, 2),
                                             dtype=np.int32)

    nc, ctx = build()
    in_maps = []
    for core in range(8):
        b, half = core // 2, core % 2
        xs = np.ascontiguousarray(x[b, 2048 * half:2048 * (half + 1), :])
        in_maps.append({
            "x0": xs, "w1t": w1tv, "w2t": w2tv, "h1t": h1tv, "h2t": h2tv,
            "w3t": w3tv, "wqkv": wqkvv_qk, "wqkvv": wqkvv_v, "wproj": wprojv,
            "s1": s1v, "s2": s2v, "hs1": hs1v, "hs2": hs2v, "s3row": s3rowv,
            "bqkv": bqkvv_, "bqkvrow": bqkvrow, "bprojrow": bprojrow,
            "mns": mns, "mst": mst, "eye": eye, "eye64": eye64,
            "nonce": nonce,
        })
    res = run_bass_kernel_spmd(nc, in_maps, core_ids=list(range(8)))
    try:
        ctx.close()
    except Exception:
        pass
    out = np.empty((B, T, Cm), dtype=np.float32)
    for core in range(8):
        b, half = core // 2, core % 2
        out[b, 2048 * half:2048 * (half + 1), :] = res.results[core]["yout"]
    if DEBUG_STAGE != 5:
        kernel.last_results = res
    return out


# version 09
# speedup vs baseline: 1.9458x; 1.9458x over previous
import numpy as np
import concourse.bass as bass
import concourse.mybir as mybir
from concourse.bass_utils import run_bass_kernel_spmd

F32 = mybir.dt.float32
BF16 = mybir.dt.bfloat16
I32 = mybir.dt.int32
AF = mybir.ActivationFunctionType
AX = mybir.AxisListType.X

DEBUG_STAGE = 5
ATTN_HEADS = 8
ATTN_GQ = 16
ATTN_PART = 9
WARMERS = False

T_LOC = 2048
C = 1024
H = 4096
NTC = 16
EPS = 1e-5

A_XQK = 0
A_XV = 32768
A_Y = 49408
ARENA_COLS = 65792
PT_R1, PT_R2, PT_R3 = 0, 2048, 4096


def build():
    nc = bass.Bass("TRN2", target_bir_lowering=False,
                   detect_race_conditions=(DEBUG_STAGE < 4))

    def din(name, shape, dt=F32):
        return nc.dram_tensor(name, list(shape), dt, kind="ExternalInput").ap()

    x0 = din("x0", [T_LOC, C])
    w1t = din("w1t", [32, 128, C]); w2t = din("w2t", [32, 128, C])
    h1t = din("h1t", [32, 128, H]); h2t = din("h2t", [32, 128, H])
    w3t = din("w3t", [H, C])
    wqkv = din("wqkv", [16, 128, C])
    wqkvv = din("wqkvv", [8, 128, C])
    wproj = din("wproj", [8, 128, C])
    s1d = din("s1", [128, 32]); s2d = din("s2", [128, 32])
    hs1d = din("hs1", [128, 32]); hs2d = din("hs2", [128, 32])
    s3rowd = din("s3row", [1, C])
    bqkvd = din("bqkv", [128, 24])
    bqkvrowd = din("bqkvrow", [1, 3 * C])
    bprojrowd = din("bprojrow", [1, C])
    mnsd = din("mns", [128, 2048]); mstd = din("mst", [128, 2048])
    eyed = din("eye", [128, 128])
    eye64d = din("eye64", [128, 64])
    nonced = din("nonce", [1, 2], I32)

    yout = nc.dram_tensor("yout", [T_LOC, C], F32, kind="ExternalOutput").ap()
    dbg = {}
    if DEBUG_STAGE == 1:
        dbg["u1"] = nc.dram_tensor("dbg_u1", [128, 8 * T_LOC], F32,
                                   kind="ExternalOutput").ap()
    if DEBUG_STAGE == 3:
        dbg["y"] = nc.dram_tensor("dbg_y", [128, 4 * 4096], F32,
                                  kind="ExternalOutput").ap()
        dbg["xp"] = nc.dram_tensor("dbg_xp", [128, 4 * 2048], F32,
                                   kind="ExternalOutput").ap()

    x1_d = nc.dram_tensor("x1_d", [T_LOC, C], F32, kind="Internal").ap()
    x2_d = nc.dram_tensor("x2_d", [T_LOC, C], F32, kind="Internal").ap()
    h2_d = nc.dram_tensor("h2_d", [32, 4, 128, T_LOC // 4], BF16,
                          kind="Internal").ap()
    h3_d = nc.dram_tensor("h3_d", [32, 4, 128, T_LOC // 4], BF16,
                          kind="Internal").ap()
    sh_y = nc.dram_tensor("sh_y", [2, 4, 128, 2048], BF16, kind="Internal",
                          addr_space="Shared").ap()
    sh_flag = nc.dram_tensor("sh_flag", [2, 1, 2], I32, kind="Internal",
                             addr_space="Shared").ap()

    from contextlib import ExitStack
    ctx = ExitStack()
    sbt = lambda n, s, d: ctx.enter_context(nc.sbuf_tensor(n, s, d))
    pst = lambda n, s, d: ctx.enter_context(nc.psum_tensor(n, s, d))

    AR = sbt("AR", [128, ARENA_COLS], BF16)
    BU = sbt("BU", [128, 8 * T_LOC], BF16)
    WB = sbt("WB", [128, 8192], BF16)
    MK = sbt("MK", [128, 4096], BF16)
    XB = sbt("XB", [128, 2048], F32)
    PT = WB
    UT = sbt("UT", [128, 2048], BF16)
    STt = sbt("STt", [128, 64], F32)
    s1t = sbt("s1t", [128, 32], F32); s2t = sbt("s2t", [128, 32], F32)
    hs1t = sbt("hs1t", [128, 32], F32); hs2t = sbt("hs2t", [128, 32], F32)
    bqkvt = sbt("bqkvt", [128, 24], F32)
    rowbuf = sbt("rowbuf", [1, C], BF16)
    s3bc = sbt("s3bc", [128, C], BF16)
    vbias = sbt("vbias", [128, C], BF16)
    eyet = sbt("eyet", [128, 128], BF16)
    eye64t = sbt("eye64t", [128, 64], BF16)
    qstage = sbt("qstage", [128, 256], BF16)
    ones1 = sbt("ones1", [1, 128], BF16)
    ones64b = sbt("ones64b", [128, 64], F32)
    onescol = sbt("onescol", [128, 1], BF16)
    ripr2 = sbt("ripr2", [128, 256], F32)
    ripb = sbt("ripb", [64, 256], F32)
    ytmp = sbt("ytmp", [128, 256], BF16)
    noncet = sbt("noncet", [1, 2], I32)
    flagt = sbt("flagt", [1, 2], I32)

    psA = pst("psA", [128, 2048], F32)
    psB = pst("psB", [128, 1024], F32)
    psD = pst("psD", [128, 512], F32)
    psE = pst("psE", [128, 512], BF16)

    sems = {}
    cnt = {}
    for s in ["SW", "FW", "SA", "FA", "SM", "FP", "FE", "SE", "SV", "SS",
              "SG", "PS", "SW0", "SW1", "SB0", "SB1", "SX0", "SX1", "SH",
              "SHG",
              "SU0", "SU1", "SU2", "SU3", "SD0", "SD1", "FPV", "FPS"]:
        sems[s] = ctx.enter_context(nc.semaphore(s))
        cnt[s] = 0

    blk_guard = nc.Block()
    blk_guard.__enter__()
    g = nc.gpsimd; sy = nc.sync; te = nc.tensor; sc = nc.scalar; ve = nc.vector

    def I(instr, s, n):
        cnt[s] += n
        instr.then_inc(sems[s], n)
        return cnt[s]

    def W(eng, s, val=None):
        v = cnt[s] if val is None else val
        if v > 0:
            eng.wait_ge(sems[s], v)

    def temark(s):
        return I(te.nop(), s, 1)

    def WFP(eng):
        W(eng, "FPV"); W(eng, "FPS")

    def warm(pool):
        i1, p1, o1, f1 = pool.take()
        W(te, f1[0], f1[1])
        m = I(te.matmul(p1[:, o1:o1 + 64], ones1[:, :], ones1[:, 0:64],
                        start=True, stop=True), "SM", 1)
        pool.set_free(i1, "SM", m)

    I(sy.dma_start(out=s1t[:], in_=s1d[:, :]), "SA", 16)
    I(sy.dma_start(out=s2t[:], in_=s2d[:, :]), "SA", 16)
    I(sy.dma_start(out=hs1t[:], in_=hs1d[:, :]), "SA", 16)
    I(sy.dma_start(out=hs2t[:], in_=hs2d[:, :]), "SA", 16)
    I(sy.dma_start(out=bqkvt[:], in_=bqkvd[:, :]), "SA", 16)
    I(sy.dma_start(out=noncet[:], in_=nonced[:, :]), "SA", 16)
    I(g.dma_start(out=eyet[:], in_=eyed[:, :]), "SW", 16)
    I(g.dma_start(out=eye64t[:], in_=eye64d[:, :]), "SW", 16)
    I(g.dma_start(out=MK[:, 0:2048], in_=mnsd[:, :]), "SW", 16)
    I(g.dma_start(out=MK[:, 2048:4096], in_=mstd[:, :]), "SW", 16)
    I(g.dma_start(out=rowbuf[:], in_=s3rowd[:, :]), "SW", 16)

    W(ve, "SW"); W(ve, "SA")
    e_st0 = I(ve.memset(STt[:], 0.0), "SV", 1)
    W(ve, "SV", e_st0)
    ve.memset(ones1[:], 1.0)
    ve.memset(ones64b[64:65, :], 1.0)
    ve.memset(onescol[:], 1.0)
    ve.memset(STt[:, 27:28], float(EPS))
    e_bq8 = I(ve.tensor_scalar_mul(STt[:, 18:26], bqkvt[:, 0:8], 0.125), "SV", 1)
    e_s3 = I(ve.tensor_scalar_mul(rowbuf[:], rowbuf[:], 0.5), "SV", 1)
    W(te, "SV", e_s3)
    for hh in range(2):
        m = I(te.matmul(psD[:, 0:512], ones1[:, :], rowbuf[:, 512 * hh:512 * (hh + 1)],
                        start=True, stop=True), "SM", 1)
        W(ve, "SM", m)
        f = I(ve.tensor_copy(s3bc[:, 512 * hh:512 * (hh + 1)], psD[:, 0:512]),
              "FE", 1)
        W(te, "FE", f)
    consts_sv = cnt["SV"]

    def bcast_row_f32(row_bf, pool, sw_ready=0):
        for hh in range(2):
            i1, p1, o1, f1 = pool.take()
            W(te, f1[0], f1[1])
            if hh == 0 and sw_ready:
                W(te, "SW", sw_ready)
            m = I(te.matmul(p1[:, o1:o1 + 512], ones1[:, :],
                            row_bf[:, 512 * hh:512 * (hh + 1)],
                            start=True, stop=True), "SM", 1)
            W(ve, "SM", m)
            f = I(ve.tensor_copy(vbias[:, 512 * hh:512 * (hh + 1)],
                                 p1[:, o1:o1 + 512]), "FPV", 1)
            pool.set_free(i1, "FPV", f)

    def ln_phase(src_ap, pool=None):
        src_t = src_ap.rearrange("(tc p) c -> tc p c", p=128)
        xfree = {}
        utfree = {}
        W(sy, "SV")
        W(sy, "SE")
        WFP(sy)
        W(sy, "SS")
        W(sy, "SD0"); W(sy, "SD1")
        for tc in range(NTC):
            slot = tc % 2
            if tc - 2 in xfree:
                W(sy, "SE", xfree[tc - 2])
            r = I(sy.dma_start(out=XB[:, 1024 * slot:1024 * (slot + 1)],
                               in_=src_t[tc]), "SX%d" % slot, 16)
            if pool is not None and WARMERS:
                warm(pool)
            xs = XB[:, 1024 * slot:1024 * (slot + 1)]
            sq = AR[:, 1024 * slot:1024 * (slot + 1)]
            col = 9 * slot
            st_ = lambda k: STt[:, col + k:col + k + 1]
            W(ve, "SX%d" % slot, r)
            e_s1 = I(ve.reduce_sum(st_(0), xs, axis=AX, negate=True), "SV", 1)
            W(sc, "SX%d" % slot, r)
            e_sq = I(sc.activation(sq, xs, AF.Square, accum_out=st_(1)), "SE", 1)
            W(ve, "SV", e_s1)
            e1 = I(ve.tensor_scalar_mul(st_(2), st_(0), 1.0 / C), "SV", 1)
            W(ve, "SE", e_sq)
            e2 = I(ve.tensor_scalar_mul(st_(3), st_(1), 1.0 / C), "SV", 1)
            W(ve, "SV", e1)
            e3 = I(ve.tensor_mul(st_(4), st_(2), st_(2)), "SV", 1)
            W(ve, "SV", e3)
            e4 = I(ve.tensor_sub(st_(5), st_(3), st_(4)), "SV", 1)
            W(sc, "SV", e4)
            e5 = I(sc.activation(st_(6), st_(5), AF.Sqrt, bias=STt[:, 27:28]), "SE", 1)
            W(ve, "SE", e5)
            e6 = I(ve.reciprocal(st_(7), st_(6)), "SV", 1)
            W(ve, "SV", e6)
            e7 = I(ve.tensor_mul(st_(8), st_(2), st_(7)), "SV", 1)
            W(sc, "SV", e7)
            if tc - 2 in utfree:
                W(sc, "SM", utfree[tc - 2])
            ut = UT[:, 1024 * slot:1024 * (slot + 1)]
            e8 = I(sc.activation(ut, xs, AF.Identity, bias=st_(8), scale=st_(7)),
                   "SE", 1)
            xfree[tc] = e8
            W(te, "SE", e8)
            for wave in range(2):
                W(te, "FE")
                mlast = None
                for q in range(4):
                    cc = 4 * wave + q
                    mlast = te.transpose(psE[:, 128 * q:128 * (q + 1)],
                                         ut[:, 128 * cc:128 * (cc + 1)], eyet[:])
                m = I(mlast, "SM", 1)
                W(ve, "SM", m)
                outv = BU[:, :].rearrange("p (cc t) -> p cc t", t=T_LOC)[
                    :, 4 * wave:4 * wave + 4, 128 * tc:128 * (tc + 1)]
                ev = I(ve.tensor_copy(outv, psE[:, 0:512].rearrange(
                    "p (cc i) -> p cc i", i=128)), "FE", 1)
            utfree[tc] = m
        return ("FE", cnt["FE"])

    PSLOTS = ([(psA, 512 * k) for k in range(4)] +
              [(psB, 512 * k) for k in range(2)] + [(psD, 0)])

    class SlotPool:
        def __init__(self):
            self.free_at = [("FPV", cnt["FPV"])] * len(PSLOTS)
            self.rr = 0

        def take(self):
            i = self.rr % len(PSLOTS)
            self.rr += 1
            return i, PSLOTS[i][0], PSLOTS[i][1], self.free_at[i]

        def set_free(self, i, sem, v):
            self.free_at[i] = (sem, v)

    def ffn_phase(x_src, x_dst):
        pool = SlotPool()
        u_done = ln_phase(x_src, pool)
        ut_rr = [0]
        ut_freeA = {}
        ut_free = {}

        wready = {}

        def load_w12(j):
            sname = "SW%d" % (j % 2)
            s_ = j % 2
            I(g.dma_start(out=WB[:, 1024 * s_:1024 * (s_ + 1)],
                          in_=w1t[j]), sname, 16)
            r2 = I(g.dma_start(out=WB[:, 1024 * (2 + s_):1024 * (3 + s_)],
                               in_=w2t[j]), sname, 16)
            wready[j] = r2

        W(g, "SM")
        load_w12(0); load_w12(1)

        for j in range(32):
            W(te, "SW%d" % (j % 2), wready[j])
            if j == 0:
                W(te, u_done[0], u_done[1])
            w1s = WB[:, 1024 * (j % 2):1024 * (j % 2 + 1)]
            w2s = WB[:, 1024 * (2 + j % 2):1024 * (3 + j % 2)]
            for tq in range(4):
                i1, p1, o1, f1 = pool.take()
                i2, p2, o2, f2 = pool.take()
                W(te, f1[0], f1[1]); W(te, f2[0], f2[1])
                for ck in range(8):
                    te.matmul(p1[:, o1:o1 + 512], w1s[:, 128 * ck:128 * (ck + 1)],
                              BU[:, 2048 * ck + 512 * tq:2048 * ck + 512 * (tq + 1)],
                              start=(ck == 0), stop=(ck == 7))
                mm = None
                for ck in range(8):
                    mm = te.matmul(p2[:, o2:o2 + 512],
                                   w2s[:, 128 * ck:128 * (ck + 1)],
                                   BU[:, 2048 * ck + 512 * tq:2048 * ck + 512 * (tq + 1)],
                                   start=(ck == 0), stop=(ck == 7))
                mgrp = I(mm, "SM", 1)
                u1 = 512 * (ut_rr[0] % 4); ut_rr[0] += 1
                u2 = 512 * (ut_rr[0] % 4); ut_rr[0] += 1
                W(sc, "SM", mgrp)
                if u1 in ut_freeA:
                    W(sc, "FPV", ut_freeA[u1])
                if u2 in ut_freeA:
                    W(sc, "FPV", ut_freeA[u2])
                ea = I(sc.activation(UT[:, u1:u1 + 512], p1[:, o1:o1 + 512],
                                     AF.Silu, scale=s1t[:, j:j + 1]), "SE", 1)
                eb = I(sc.activation(UT[:, u2:u2 + 512], p2[:, o2:o2 + 512],
                                     AF.Copy, scale=s2t[:, j:j + 1]), "SE", 1)
                W(ve, "SE", eb)
                f = I(ve.tensor_mul(AR[:, 2048 * j + 512 * tq:2048 * j + 512 * (tq + 1)],
                                    UT[:, u1:u1 + 512], UT[:, u2:u2 + 512]), "FPV", 2)
                ut_freeA[u1] = f; ut_freeA[u2] = f
                pool.set_free(i1, "FPV", f); pool.set_free(i2, "FPV", f)
            if j + 2 < 32:
                W(g, "SM", mgrp)
                load_w12(j + 2)

        def big_mid(win, wsc_t, src_is_arena, src_d, dst_d):
            wr = {}

            def load_big(j):
                s_ = j % 2
                wr[j] = I(g.dma_start(out=WB[:, 4096 * s_:4096 * (s_ + 1)],
                                      in_=win[j]), "SB%d" % (j % 2), 16)

            W(g, "SM")
            load_big(0); load_big(1)
            if not src_is_arena:
                lded = 0
                for jj in range(32):
                    lded = I(sy.dma_start(
                        out=AR[:, 2048 * jj:2048 * (jj + 1)].rearrange(
                            "p (t c) -> p t c", t=4),
                        in_=src_d[jj].rearrange("t p c -> p t c")), "SH", 16)
            for j2 in range(32):
                W(te, "SB%d" % (j2 % 2), wr[j2])
                if j2 == 0:
                    if src_is_arena:
                        WFP(te)
                    else:
                        W(te, "SH", lded)
                ws = WB[:, 4096 * (j2 % 2):4096 * (j2 % 2 + 1)]
                for tq in range(4):
                    i1, p1, o1, f1 = pool.take()
                    W(te, f1[0], f1[1])
                    mm = None
                    for ck in range(32):
                        mm = te.matmul(
                            p1[:, o1:o1 + 512], ws[:, 128 * ck:128 * (ck + 1)],
                            AR[:, 2048 * ck + 512 * tq:2048 * ck + 512 * (tq + 1)],
                            start=(ck == 0), stop=(ck == 31))
                    mgrp = I(mm, "SM", 1)
                    uslot = ut_rr[0] % 4
                    uo = 512 * uslot; ut_rr[0] += 1
                    W(sc, "SM", mgrp)
                    if uslot in ut_free:
                        W(sc, "SU%d" % uslot, ut_free[uslot])
                    e = I(sc.activation(UT[:, uo:uo + 512], p1[:, o1:o1 + 512],
                                        AF.Gelu, scale=wsc_t[:, j2:j2 + 1]), "FPS", 1)
                    pool.set_free(i1, "FPS", e)
                    W(sy, "FPS", e)
                    ut_free[uslot] = I(sy.dma_start(
                        out=dst_d[j2, tq],
                        in_=UT[:, uo:uo + 512]), "SU%d" % uslot, 16)
                if j2 + 2 < 32:
                    W(g, "SM", mgrp)
                    load_big(j2 + 2)
            return None

        big_mid(h1t, hs1t, True, None, h2_d)
        for u_ in range(4):
            W(sy, "SU%d" % u_)
        big_mid(h2t, hs2t, False, h2_d, h3_d)
        for u_ in range(4):
            W(sy, "SU%d" % u_)

        lded = 0
        for jj in range(32):
            lded = I(sy.dma_start(
                out=AR[:, 2048 * jj:2048 * (jj + 1)].rearrange(
                    "p (t c) -> p t c", t=4),
                in_=h3_d[jj].rearrange("t p c -> p t c")), "SH", 16)
        W(sy, "FA")
        x_src_t = x_src.rearrange("(tc p) c -> tc p c", p=128)
        x_dst_t = x_dst.rearrange("(tc p) c -> tc p c", p=128)
        w3v = w3t.rearrange("(k p) m -> k p m", p=128)
        xload = {}
        xst = {}
        supers = [list(range(s, min(s + 3, NTC))) for s in range(0, NTC, 3)]
        first = True
        for sup in supers:
            wr = {}

            def load_w3(k):
                wr[k] = I(g.dma_start(out=WB[:, 1024 * (k % 2):1024 * (k % 2 + 1)],
                                      in_=w3v[k]), "SW%d" % (k % 2), 16)

            W(g, "SM")
            load_w3(0); load_w3(1)
            def load_x(ti):
                prev = ti - 2
                if prev in xst:
                    W(sy, "SD%d" % (prev % 2), xst[prev])
                xload[ti] = I(sy.dma_start(
                    out=XB[:, 1024 * (ti % 2):1024 * (ti % 2 + 1)],
                    in_=x_src_t[ti]), "SX%d" % (ti % 2), 16)

            for ti in sup[:2]:
                load_x(ti)
            ts = {}
            for ti in sup:
                a = pool.take(); b = pool.take()
                W(te, a[3][0], a[3][1]); W(te, b[3][0], b[3][1])
                ts[ti] = (a, b)
            if first:
                W(te, "SH", lded)
                first = False
            last_mm = None
            for k in range(32):
                W(ve, "SW%d" % (k % 2), wr[k])
                wvs = WB[:, 1024 * (k % 2):1024 * (k % 2 + 1)]
                esc = I(ve.tensor_mul(wvs, wvs, s3bc[:]), "SV", 1)
                W(te, "SV", esc)
                for ti in sup:
                    (ia, pa, oa, _), (ib, pb, ob, _) = ts[ti]
                    te.matmul(pa[:, oa:oa + 512],
                              AR[:, 2048 * k + 128 * ti:2048 * k + 128 * (ti + 1)],
                              wvs[:, 0:512], start=(k == 0), stop=(k == 31))
                    last_mm = te.matmul(
                        pb[:, ob:ob + 512],
                        AR[:, 2048 * k + 128 * ti:2048 * k + 128 * (ti + 1)],
                        wvs[:, 512:1024], start=(k == 0), stop=(k == 31))
                km = I(last_mm, "SM", 1)
                if k + 2 < 32:
                    W(g, "SM", km)
                    load_w3(k + 2)
            mgrp = km
            for idx, ti in enumerate(sup):
                if idx == 2:
                    load_x(ti)
                (ia, pa, oa, _), (ib, pb, ob, _) = ts[ti]
                W(ve, "SM", mgrp)
                W(ve, "SX%d" % (ti % 2), xload[ti])
                xv = XB[:, 1024 * (ti % 2):1024 * (ti % 2 + 1)]
                ve.tensor_add(xv[:, 0:512], pa[:, oa:oa + 512], xv[:, 0:512])
                f = I(ve.tensor_add(xv[:, 512:1024], pb[:, ob:ob + 512],
                                    xv[:, 512:1024]), "FPV", 2)
                pool.set_free(ia, "FPV", f); pool.set_free(ib, "FPV", f)
                W(sy, "FPV", f)
                xst[ti] = I(sy.dma_start(out=x_dst_t[ti], in_=xv),
                            "SD%d" % (ti % 2), 16)
        return None

    def finish(engines_extra=()):
        W(sy, "SS")
        W(sy, "SD0"); W(sy, "SD1")
        W(g, "SS")
        W(g, "SW")
        W(g, "SHG")
        W(g, "SD0"); W(g, "SD1")
        blk_guard.__exit__(None, None, None)
        return nc, ctx

    def copy_dram_to_out(src, extra_done):
        st = src.rearrange("(tc p) c -> tc p c", p=128)
        yt = yout.rearrange("(tc p) c -> tc p c", p=128)
        W(sy, "SS", extra_done)
        W(sy, "SE"); W(sy, "SV"); WFP(sy)
        for tc in range(NTC):
            r = I(sy.dma_start(out=XB[:, 0:1024], in_=st[tc]), "SA", 16)
            W(sy, "SA", r)
            s = I(sy.dma_start(out=yt[tc], in_=XB[:, 0:1024]), "SS", 16)
            W(sy, "SS", s)

    if DEBUG_STAGE == 1:
        u_done = ln_phase(x0)
        W(g, "SV", u_done)
        I(g.dma_start(out=dbg["u1"][:, :], in_=BU[:, :]), "SW", 16)
        W(sy, "SV", u_done)
        copy_dram_to_out(x0, 0)
        return finish()

    dst1 = yout if DEBUG_STAGE == 2 else x1_d
    ffn_phase(x0, dst1)
    if DEBUG_STAGE == 2:
        return finish()

    W(sy, "SD0"); W(sy, "SD1")
    pool = SlotPool()
    u2_done = ln_phase(x1_d, pool)

    wr = {}

    def load_qk(oc):
        s_ = oc % 2
        wr[oc] = I(g.dma_start(out=WB[:, 1024 * s_:1024 * (s_ + 1)],
                               in_=wqkv[oc]), "SW%d" % (oc % 2), 16)

    W(g, "SM")
    load_qk(0); load_qk(1)
    for oc in range(16):
        W(te, "SW%d" % (oc % 2), wr[oc])
        if oc == 0:
            W(te, u2_done[0], u2_done[1])
        ws = WB[:, 1024 * (oc % 2):1024 * (oc % 2 + 1)]
        for tq in range(4):
            i1, p1, o1, f1 = pool.take()
            W(te, f1[0], f1[1])
            mm = None
            for ck in range(8):
                mm = te.matmul(
                    p1[:, o1:o1 + 512], ws[:, 128 * ck:128 * (ck + 1)],
                    BU[:, 2048 * ck + 512 * tq:2048 * ck + 512 * (tq + 1)],
                    start=(ck == 0), stop=(ck == 7))
            mgrp = I(mm, "SM", 1)
            W(sc, "SM", mgrp)
            if oc < 8:
                bias_ap = STt[:, 18 + oc:19 + oc]
                scl = 0.125
            else:
                bias_ap = bqkvt[:, oc:oc + 1]
                scl = 1.0
            e = I(sc.activation(
                AR[:, A_XQK + 2048 * oc + 512 * tq:A_XQK + 2048 * oc + 512 * (tq + 1)],
                p1[:, o1:o1 + 512], AF.Identity, bias=bias_ap, scale=scl), "FPS", 1)
            pool.set_free(i1, "FPS", e)
        if oc + 2 < 16:
            W(g, "SM", mgrp)
            load_qk(oc + 2)

    W(g, "FE")
    rb_v = I(g.dma_start(out=rowbuf[:], in_=bqkvrowd[:, 2048:3072]), "SW", 16)
    bcast_row_f32(rowbuf, pool, sw_ready=rb_v)
    W(g, "FPV")
    W(g, "SM")
    vrdy = 0
    for cc in range(8):
        vrdy = I(g.dma_start(out=WB[:, 1024 * cc:1024 * (cc + 1)],
                             in_=wqkvv[cc]), "SW", 16)
    W(ve, "SM")
    for tc in range(NTC):
        base = A_XV + 1040 * tc
        vv = AR[:, base:base + 1040].rearrange("p (gg e) -> p gg e", e=65)
        I(ve.memset(vv[:, :, 64:65], 1.0), "SV", 1)
    vb_sv = cnt["FPV"]
    for ti in range(NTC):
        i1, p1, o1, f1 = pool.take()
        i2, p2, o2, f2 = pool.take()
        W(te, f1[0], f1[1]); W(te, f2[0], f2[1])
        if ti == 0:
            W(te, "SW", vrdy)
        mm = None
        for cc in range(8):
            te.matmul(p1[:, o1:o1 + 512],
                      BU[:, 2048 * cc + 128 * ti:2048 * cc + 128 * (ti + 1)],
                      WB[:, 1024 * cc:1024 * cc + 512],
                      start=(cc == 0), stop=(cc == 7))
            mm = te.matmul(p2[:, o2:o2 + 512],
                           BU[:, 2048 * cc + 128 * ti:2048 * cc + 128 * (ti + 1)],
                           WB[:, 1024 * cc + 512:1024 * (cc + 1)],
                           start=(cc == 0), stop=(cc == 7))
        mgrp = I(mm, "SM", 1)
        W(ve, "SM", mgrp)
        W(ve, "FPV", vb_sv)
        base = A_XV + 1040 * ti
        ov = AR[:, base:base + 1040].rearrange("p (gg e) -> p gg e", e=65)
        ve.tensor_add(ov[:, 0:8, 0:64],
                      p1[:, o1:o1 + 512].rearrange("p (gg e) -> p gg e", e=64),
                      vbias[:, 0:512].rearrange("p (gg e) -> p gg e", e=64))
        f = I(ve.tensor_add(ov[:, 8:16, 0:64],
                            p2[:, o2:o2 + 512].rearrange("p (gg e) -> p gg e", e=64),
                            vbias[:, 512:1024].rearrange("p (gg e) -> p gg e", e=64)),
              "FPV", 2)
        pool.set_free(i1, "FPV", f); pool.set_free(i2, "FPV", f)

    sq_used = {}
    for cc in range(16):
        sl = cc % 2
        sq = AR[:, A_Y + 2048 * sl:A_Y + 2048 * (sl + 1)]
        if sl in sq_used:
            W(sc, "SM", sq_used[sl])
        e = I(sc.activation(sq, AR[:, A_XQK + 2048 * cc:A_XQK + 2048 * (cc + 1)],
                            AF.Square), "SE", 1)
        W(te, "SE", e)
        takes = [pool.take() for _ in range(4)]
        for tk_ in takes:
            W(te, tk_[3][0], tk_[3][1])
        mm = None
        for q in range(4):
            i_, p_, o_, _ = takes[q]
            mm = te.matmul(p_[0:1, o_:o_ + 512], onescol[:, :],
                           sq[:, 512 * q:512 * (q + 1)],
                           start=True, stop=True)
            I(mm, "SM", 1)
        mgrp = cnt["SM"]
        sq_used[sl] = mgrp
        W(ve, "SM", mgrp)
        W(ve, "SV")
        for q in range(4):
            i_, p_, o_, _ = takes[q]
            f = I(ve.reduce_max(STt[0:1, 36 + q:37 + q], p_[0:1, o_:o_ + 512],
                                axis=AX), "FPV", 1)
            pool.set_free(i_, "FPV", f)
        W(ve, "FPV")
        I(ve.reduce_max(STt[0:1, 40 + cc:41 + cc], STt[0:1, 36:40], axis=AX),
          "SV", 1)
    W(ve, "FPV")
    W(ve, "SV")
    ej = I(ve.reduce_max(STt[0:1, 56:57], STt[0:1, 40:48], axis=AX), "SV", 1)
    W(ve, "SV", ej)
    ek = I(ve.reduce_max(STt[0:1, 57:58], STt[0:1, 48:56], axis=AX), "SV", 1)
    W(sc, "SV", ek)
    es1 = I(sc.activation(STt[0:1, 58:59], STt[0:1, 56:57], AF.Sqrt), "SE", 1)
    es2 = I(sc.activation(STt[0:1, 59:60], STt[0:1, 57:58], AF.Sqrt), "SE", 1)
    W(ve, "SE", es2)
    em = I(ve.tensor_mul(STt[0:1, 60:61], STt[0:1, 58:59], STt[0:1, 59:60]),
           "SV", 1)
    W(ve, "SV", em)
    em2 = I(ve.tensor_scalar_mul(STt[0:1, 61:62], STt[0:1, 60:61], -1.0), "SV", 1)
    W(ve, "SV", em2)
    eb = I(ve.tensor_copy(UT[0:1, 1:2], STt[0:1, 61:62]), "SV", 1)
    W(te, "SV", eb)
    i1, p1, o1, f1 = pool.take()
    W(te, f1[0], f1[1])
    mb = I(te.matmul(p1[:, o1:o1 + 1], ones1[:, :], UT[0:1, 1:2], start=True,
                     stop=True), "SM", 1)
    W(sc, "SM", mb)
    e_nm = I(sc.activation(STt[:, 26:27], p1[:, o1:o1 + 1], AF.Copy), "FPS", 1)
    pool.set_free(i1, "FPS", e_nm)
    negmh = STt[:, 26:27]

    def ydst_even(ytile, g_q):
        return ytile[0:64, :].rearrange("p (a sx) -> p a sx", sx=16)[:, :, g_q]

    def ydst_odd(ytile, g_q):
        return ytile[64:128, :].rearrange("p (a sx) -> p a sx", sx=16)[:, :, g_q]

    SGRP = [(psA, 0), (psA, 1024), (psB, 0)]
    rot = {"c": 0}
    grp_free = [("SE", 0)] * 3
    pipe = {"mv": {}, "masks": {}, "eq": {}, "pend": None}

    def ptb(g):
        return PT if g % 2 == 0 else BU

    def emit_rebase(h_loc, g_q):
        t0 = 256 * h_loc
        qcc = g_q // 2
        qof = 64 * (g_q % 2)
        qof2 = 64 - qof
        rhs_q = AR[qof:qof + 64,
                   A_XQK + 2048 * qcc + t0:A_XQK + 2048 * qcc + t0 + 256]
        W(te, "SE", cnt["SE"])
        WFP(te)
        mq = I(te.matmul(psD[qof2:qof2 + 64, 256:512],
                         eye64t[qof:qof + 64, :], rhs_q,
                         start=True, stop=True), "SM", 1)
        W(ve, "SM", mq)
        eq = I(ve.tensor_copy(qstage[qof2:qof2 + 64, :],
                              psD[qof2:qof2 + 64, 256:512]), "SV", 1)
        pipe["eq"][(h_loc, g_q)] = eq
        return rhs_q, qstage[qof2:qof2 + 64, :], qof

    def emit_s(h_loc, g_q, rhs_q, rhs_q2, qof):
        t0 = 256 * h_loc
        PTg = ptb(g_q)
        W(te, "SV", pipe["eq"][(h_loc, g_q)])
        dbg3 = (DEBUG_STAGE == 3)
        for fill in range(0 if not (dbg3 and ATTN_PART < 2) else 4, 4):
            gi = rot["c"] % 3
            rot["c"] += 1
            gten, gbase = SGRP[gi]
            W(te, grp_free[gi][0], grp_free[gi][1])
            lm = None
            for q in range(4):
                g_k = 4 * fill + 2 * (q // 2) + (q % 2)
                co = gbase + 512 * (q % 2) + 256 * (q // 2)
                kcc = 8 + g_k // 2
                kof = 64 * (g_k % 2)
                lhs_k = AR[kof:kof + 64,
                           A_XQK + 2048 * kcc + t0:A_XQK + 2048 * kcc + t0 + 128]
                rq = rhs_q if kof == qof else rhs_q2
                lm = te.matmul(gten[:, co:co + 256],
                               lhs_k, rq, start=True, stop=True)
            m = I(lm, "SM", 1)
            W(sc, "SM", m)
            dst5 = PTg[:, PT_R1:PT_R1 + 4096].rearrange(
                "p (hh gkh gkl i) -> p gkh gkl hh i", hh=2, gkh=8, gkl=2, i=128)
            e = None
            for bk in range(2):
                dst = dst5[:, 2 * fill:2 * fill + 2, bk, :, :]
                srcv = gten[:, gbase + 512 * bk:gbase + 512 * (bk + 1)
                            ].rearrange("p (q2 hh i) -> p q2 hh i",
                                        q2=2, hh=2, i=128)
                e = I(sc.activation(dst, srcv, AF.Exp, bias=negmh), "SE", 1)
            grp_free[gi] = ("SE", e)
        for fill in range(0 if not (dbg3 and ATTN_PART < 3) else 4, 4):
            gi = rot["c"] % 3
            rot["c"] += 1
            gten, gbase = SGRP[gi]
            W(te, grp_free[gi][0], grp_free[gi][1])
            lm = None
            for q in range(4):
                g_k = 4 * fill + 2 * (q // 2) + (q % 2)
                co = gbase + 512 * (q % 2) + 256 * (q // 2)
                kcc = 8 + g_k // 2
                kof = 64 * (g_k % 2)
                lhs_k = AR[kof:kof + 64,
                           A_XQK + 2048 * kcc + t0 + 128:
                           A_XQK + 2048 * kcc + t0 + 256]
                rq = rhs_q if kof == qof else rhs_q2
                lm = te.matmul(gten[:, co:co + 128],
                               lhs_k, rq[:, 128:256], start=True, stop=True)
            m = I(lm, "SM", 1)
            W(sc, "SM", m)
            d3 = PTg[:, PT_R3 + 512 * fill:PT_R3 + 512 * (fill + 1)].rearrange(
                "p (q2 gkl i) -> p q2 gkl i", q2=2, gkl=2, i=128)
            e = None
            for bk in range(2):
                dst = d3[:, :, bk, :]
                srcv = gten[:, gbase + 512 * bk:gbase + 512 * (bk + 1)
                            ].rearrange("p (q2 i) -> p q2 i",
                                        q2=2, i=256)[:, :, 0:128]
                e = I(sc.activation(dst, srcv, AF.Exp, bias=negmh), "SE", 1)
            grp_free[gi] = ("SE", e)
        if dbg3 and ATTN_PART < 4:
            pipe["masks"][(h_loc, g_q)] = cnt["SV"]
            return
        W(ve, "SE")
        nsw = 128 * (g_q + 1)
        I(ve.tensor_mul(PTg[:, PT_R1:PT_R1 + nsw], PTg[:, PT_R1:PT_R1 + nsw],
                        MK[:, 0:nsw]), "SV", 1)
        if nsw < 2048:
            I(ve.tensor_mul(PTg[:, PT_R1 + nsw:PT_R1 + 2048],
                            PTg[:, PT_R1 + nsw:PT_R1 + 2048],
                            MK[:, 2048:4096 - nsw]), "SV", 1)
        I(ve.tensor_mul(PTg[:, PT_R3:PT_R3 + nsw], PTg[:, PT_R3:PT_R3 + nsw],
                        MK[:, 0:nsw]), "SV", 1)
        if nsw < 2048:
            I(ve.tensor_mul(PTg[:, PT_R3 + nsw:PT_R3 + 2048],
                            PTg[:, PT_R3 + nsw:PT_R3 + 2048],
                            MK[:, 2048:4096 - nsw]), "SV", 1)
        pipe["masks"][(h_loc, g_q)] = cnt["SV"]

    def emit_av_norm(h_loc, g_q, eq_succ=None):
        t0 = 256 * h_loc
        PTg = ptb(g_q)
        W(te, "SV", pipe["masks"][(h_loc, g_q)])
        if eq_succ is not None:
            W(te, "SV", eq_succ)
        WFP(te)
        lm = None
        tc0 = 2 * h_loc
        tc1 = 2 * h_loc + 1
        for g_k in range(16):
            lhsv = AR[:, A_XV + 1040 * tc0 + 65 * g_k:
                      A_XV + 1040 * tc0 + 65 * (g_k + 1)]
            lm = te.matmul(psD[0:65, 0:128], lhsv,
                           PTg[:, PT_R1 + 128 * g_k:PT_R1 + 128 * (g_k + 1)],
                           start=(g_k == 0), stop=(g_k == 15))
        for g_k in range(16):
            lhsv = AR[:, A_XV + 1040 * tc0 + 65 * g_k:
                      A_XV + 1040 * tc0 + 65 * (g_k + 1)]
            te.matmul(psD[0:65, 128:256], lhsv,
                      PTg[:, PT_R2 + 128 * g_k:PT_R2 + 128 * (g_k + 1)],
                      start=(g_k == 0), stop=False)
        for g_k in range(16):
            lhsv = AR[:, A_XV + 1040 * tc1 + 65 * g_k:
                      A_XV + 1040 * tc1 + 65 * (g_k + 1)]
            lm = te.matmul(psD[0:65, 128:256], lhsv,
                           PTg[:, PT_R3 + 128 * g_k:PT_R3 + 128 * (g_k + 1)],
                           start=False, stop=(g_k == 15))
        mv = I(lm, "SM", 1)
        pipe["mv"][(h_loc, g_q)] = mv
        W(ve, "SM", mv)
        er = I(ve.reciprocal(ripr2[64:65, 0:128], psD[64:65, 0:128]), "SV", 1)
        W(ve, "SV", er)
        er = I(ve.reciprocal(ripr2[64:65, 128:256], psD[64:65, 128:256]), "SV", 1)
        W(ve, "SV", er)
        W(te, "SV", er)
        mb2 = I(te.matmul(psD[0:64, 256:512], ones64b[64:65, :],
                          ripr2[64:65, :], start=True, stop=True), "SM", 1)
        W(sc, "SM", mb2)
        ec = I(sc.activation(ripb[:, :], psD[0:64, 256:512], AF.Copy), "SE", 1)
        W(ve, "SE", ec)
        ytile = AR[:, A_Y + 4096 * (h_loc // 2):A_Y + 4096 * (h_loc // 2 + 1)]
        if h_loc % 2 == 0:
            yd = ydst_even(ytile, g_q)
            I(ve.tensor_mul(yd[:, 0:128], psD[0:64, 0:128],
                            ripb[:, 0:128]), "FPV", 1)
            I(ve.tensor_mul(yd[:, 128:256], psD[0:64, 128:256],
                            ripb[:, 128:256]), "FPV", 1)
        else:
            I(ve.tensor_mul(ytmp[0:64, 0:128], psD[0:64, 0:128],
                            ripb[:, 0:128]), "SV", 1)
            en = I(ve.tensor_mul(ytmp[0:64, 128:256], psD[0:64, 128:256],
                                 ripb[:, 128:256]), "SV", 1)
            W(te, "SV", en)
            mc = I(te.matmul(psD[64:128, 256:512], eyet[0:64, 0:64],
                             ytmp[0:64, :], start=True, stop=True), "SM", 1)
            W(ve, "SM", mc)
            I(ve.tensor_copy(ydst_odd(ytile, g_q), psD[64:128, 256:512]),
              "FPV", 1)

    def head_attn(h_loc):
        ngq = ATTN_GQ if DEBUG_STAGE == 3 else 16
        for g_q in range(ngq):
            rq, rq2, qof = emit_rebase(h_loc, g_q)
            emit_s(h_loc, g_q, rq, rq2, qof)
            if DEBUG_STAGE == 3 and ATTN_PART < 5:
                continue
            if pipe["pend"] is not None:
                emit_av_norm(*pipe["pend"],
                             eq_succ=pipe["eq"][(h_loc, g_q)])
            pipe["pend"] = (h_loc, g_q)

    NH = ATTN_HEADS if DEBUG_STAGE == 3 else 8
    W(te, "FPV"); W(te, "FPS")
    for h_loc in range(NH):
        head_attn(h_loc)
    if pipe["pend"] is not None and not (DEBUG_STAGE == 3 and ATTN_PART < 5):
        emit_av_norm(*pipe["pend"])
        pipe["pend"] = None
    y_done = cnt["FPV"]

    W(g, "SM")
    wprdy = 0
    for cc in range(8):
        wprdy = I(g.dma_start(out=WB[:, 1024 * cc:1024 * (cc + 1)],
                              in_=wproj[cc]), "SW", 16)

    if DEBUG_STAGE == 3:
        W(g, "FPV", y_done)
        W(g, "SV")
        I(g.dma_start(out=dbg["y"][:, :], in_=AR[:, A_Y:A_Y + 16384]), "SW", 16)
        I(g.dma_start(out=dbg["xp"][:, :], in_=AR[:, A_XQK:A_XQK + 8192]),
          "SW", 16)
        copy_dram_to_out(x1_d, 0)
        return finish()

    flag_r = g.alloc_register("flag_r"); flag_r2 = g.alloc_register("flag_r2")
    n0_r = g.alloc_register("n0_r"); n1_r = g.alloc_register("n1_r")
    cnt_r = g.alloc_register("cnt_r"); cond_r = g.alloc_register("cond_r")
    t_r = g.alloc_register("t_r")
    ss0 = cnt["SS"]
    sw0 = cnt["SW"]
    W(g, "SA")
    for arm in nc.switch_core_id(engines=[sy, g], n=128):
        p = int(arm) % 2
        q = p ^ 1
        pcols = 2048 * (1 - p)
        sy.wait_ge(sems["FPV"], y_done)
        for j in range(4):
            sy.dma_start(out=sh_y[p, j, :, :],
                         in_=AR[:, A_Y + 4096 * j + pcols:
                                A_Y + 4096 * j + pcols + 2048]
                         ).then_inc(sems["SS"], 16)
        sy.wait_ge(sems["SS"], ss0 + 64)
        sy.dma_start(out=sh_flag[p, :, :], in_=noncet[:]).then_inc(sems["SS"], 16)
        g.wait_ge(sems["SW"], sw0)
        g.reg_load(n0_r, noncet[0:1, 0:1])
        g.reg_load(n1_r, noncet[0:1, 1:2])
        g.reg_mov(cnt_r, 0)
        g.reg_mov(cond_r, 1)
        with g.While(cond_r):
            g.dma_start(out=flagt[:], in_=sh_flag[q, :, :]).then_inc(sems["PS"], 16)
            g.reg_add(cnt_r, cnt_r, 16)
            g.wait_ge(sems["PS"], cnt_r)
            g.reg_load(flag_r, flagt[0:1, 0:1])
            g.reg_load(flag_r2, flagt[0:1, 1:2])
            g.reg_alu(cond_r, flag_r, n0_r, mybir.AluOpType.not_equal)
            g.reg_alu(t_r, flag_r2, n1_r, mybir.AluOpType.not_equal)
            g.reg_alu(cond_r, cond_r, t_r, mybir.AluOpType.logical_or)
        g.nop().then_inc(sems["SG"], 1)
        sy.wait_ge(sems["SG"], 1)
        for j in range(4):
            sy.dma_start(out=AR[:, A_Y + 4096 * j + pcols:
                                A_Y + 4096 * j + pcols + 2048],
                         in_=sh_y[q, j, :, :]).then_inc(sems["SS"], 16)
    cnt["SG"] += 1
    cnt["SS"] += 64 + 16 + 64
    exch_done = cnt["SS"]

    rb_p = I(g.dma_start(out=rowbuf[:], in_=bprojrowd[:, :]), "SW", 16)
    W(ve, "SV")
    bcast_row_f32(rowbuf, pool, sw_ready=rb_p)
    bp_sv = cnt["FPV"]
    x1t = x1_d.rearrange("(tc p) c -> tc p c", p=128)
    x2t = x2_d.rearrange("(tc p) c -> tc p c", p=128)
    base_c = {k: cnt[k] for k in ("SM", "SA", "FPV", "SS", "SX0", "SX1",
                                  "SD0", "SD1")}
    for arm in nc.switch_core_id(engines=[te, sc, ve, sy], n=128):
        half = int(arm) % 2
        cc_ = dict(base_c)
        rr = 0
        sy.wait_ge(sems["FA"], cnt["FA"])
        te.wait_ge(sems["SS"], exch_done)
        te.wait_ge(sems["SW"], cnt["SW"])
        te.wait_ge(sems["FPV"], bp_sv)
        xst2 = {}
        for ti in range(NTC):
            sa = 512 * (rr % 4); sbo = 512 * ((rr + 1) % 4)
            rr += 2
            te.wait_ge(sems["FPV"], cc_["FPV"])
            lm = None
            for cc in range(8):
                j = cc % 4
                cols = 2048 * half if (cc // 4) == half else 2048 * (1 - half)
                yv = AR[:, A_Y + 4096 * j + cols + 128 * ti:
                        A_Y + 4096 * j + cols + 128 * (ti + 1)]
                te.matmul(psA[:, sa:sa + 512], yv, WB[:, 1024 * cc:1024 * cc + 512],
                          start=(cc == 0), stop=(cc == 7))
                lm = te.matmul(psA[:, sbo:sbo + 512], yv,
                               WB[:, 1024 * cc + 512:1024 * (cc + 1)],
                               start=(cc == 0), stop=(cc == 7))
            cc_["SM"] += 1
            lm.then_inc(sems["SM"], 1)
            if ti - 2 in xst2:
                sy.wait_ge(sems["SD%d" % (ti % 2)], xst2[ti - 2])
            sxn = "SX%d" % (ti % 2)
            sy.dma_start(out=XB[:, 1024 * (ti % 2):1024 * (ti % 2 + 1)],
                         in_=x1t[ti]).then_inc(sems[sxn], 16)
            cc_[sxn] += 16
            ve.wait_ge(sems["SM"], cc_["SM"])
            ve.wait_ge(sems["SX%d" % (ti % 2)], cc_["SX%d" % (ti % 2)])
            xv = XB[:, 1024 * (ti % 2):1024 * (ti % 2 + 1)]
            ve.tensor_add(xv[:, 0:512], psA[:, sa:sa + 512], xv[:, 0:512])
            ve.tensor_add(xv[:, 512:1024], psA[:, sbo:sbo + 512],
                          xv[:, 512:1024])
            ve.tensor_add(xv[:, 0:512], xv[:, 0:512], vbias[:, 0:512])
            ve.tensor_add(xv[:, 512:1024], xv[:, 512:1024],
                          vbias[:, 512:1024]).then_inc(sems["FPV"], 2)
            cc_["FPV"] += 2
            sy.wait_ge(sems["FPV"], cc_["FPV"])
            sdn = "SD%d" % (ti % 2)
            xst2[ti] = cc_[sdn] + 16
            sy.dma_start(out=x2t[ti], in_=xv).then_inc(sems[sdn], 16)
            cc_[sdn] += 16
    for k in base_c:
        cnt[k] = cc_[k]

    if DEBUG_STAGE == 4:
        W(sy, "SD0"); W(sy, "SD1")
        copy_dram_to_out(x2_d, 0)
        return finish()

    ffn_phase(x2_d, yout)
    return finish()



_CACHE = {}


def kernel(L0_attn_x, L0_attn_Wqkv, L0_attn_bqkv, L0_attn_Wproj, L0_attn_bproj,
           L0_ffn_w1_w, L0_ffn_w1_s, L0_ffn_w2_w, L0_ffn_w2_s,
           L0_ffn_h1_w, L0_ffn_h1_s, L0_ffn_h2_w, L0_ffn_h2_s,
           L0_ffn_w3_w, L0_ffn_w3_s):
    x = np.ascontiguousarray(np.asarray(L0_attn_x, dtype=np.float32))
    B, T, Cm = x.shape
    asc = lambda a: np.ascontiguousarray(np.asarray(a, dtype=np.float32))

    def tile_in_out(wT, nj):
        cin = wT.shape[0]
        return np.ascontiguousarray(
            wT.reshape(cin // 128, 128, nj, 128).transpose(2, 1, 0, 3)
            .reshape(nj, 128, cin))

    w1tv = tile_in_out(asc(np.asarray(L0_ffn_w1_w).T), 32)
    w2tv = tile_in_out(asc(np.asarray(L0_ffn_w2_w).T), 32)
    h1tv = tile_in_out(asc(np.asarray(L0_ffn_h1_w).T), 32)
    h2tv = tile_in_out(asc(np.asarray(L0_ffn_h2_w).T), 32)
    w3tv = asc(np.asarray(L0_ffn_w3_w).T)
    wq_all = asc(L0_attn_Wqkv)
    wqkvv_qk = tile_in_out(wq_all[:, 0:2048], 16)
    wqkvv_v = np.ascontiguousarray(
        wq_all[:, 2048:3072].reshape(8, 128, 1024))
    wprojv = np.ascontiguousarray(asc(L0_attn_Wproj).reshape(8, 128, 1024))
    s1v = asc(np.asarray(L0_ffn_w1_s).reshape(32, 128).T)
    s2v = asc(np.asarray(L0_ffn_w2_s).reshape(32, 128).T)
    hs1v = asc(np.asarray(L0_ffn_h1_s).reshape(32, 128).T)
    hs2v = asc(np.asarray(L0_ffn_h2_s).reshape(32, 128).T)
    s3rowv = asc(L0_ffn_w3_s).reshape(1, C)
    bqkvv_ = asc(np.asarray(L0_attn_bqkv).reshape(24, 128).T)
    bqkvrow = asc(L0_attn_bqkv).reshape(1, 3 * C)
    bprojrow = asc(L0_attn_bproj).reshape(1, C)
    p_ = np.arange(128)[:, None]
    cq = np.arange(2048)[None, :] % 128
    mns = (p_ <= cq).astype(np.float32)
    mst = (p_ < cq).astype(np.float32)
    eye = np.eye(128, dtype=np.float32)
    eye64 = np.zeros((128, 64), dtype=np.float32)
    eye64[np.arange(128), np.arange(128) % 64] = 1.0
    nonce = np.random.default_rng().integers(1, 2**31 - 1, size=(1, 2),
                                             dtype=np.int32)

    nc, ctx = build()
    in_maps = []
    for core in range(8):
        b, half = core // 2, core % 2
        xs = np.ascontiguousarray(x[b, 2048 * half:2048 * (half + 1), :])
        in_maps.append({
            "x0": xs, "w1t": w1tv, "w2t": w2tv, "h1t": h1tv, "h2t": h2tv,
            "w3t": w3tv, "wqkv": wqkvv_qk, "wqkvv": wqkvv_v, "wproj": wprojv,
            "s1": s1v, "s2": s2v, "hs1": hs1v, "hs2": hs2v, "s3row": s3rowv,
            "bqkv": bqkvv_, "bqkvrow": bqkvrow, "bprojrow": bprojrow,
            "mns": mns, "mst": mst, "eye": eye, "eye64": eye64,
            "nonce": nonce,
        })
    res = run_bass_kernel_spmd(nc, in_maps, core_ids=list(range(8)))
    try:
        ctx.close()
    except Exception:
        pass
    out = np.empty((B, T, Cm), dtype=np.float32)
    for core in range(8):
        b, half = core // 2, core % 2
        out[b, 2048 * half:2048 * (half + 1), :] = res.results[core]["yout"]
    if DEBUG_STAGE != 5:
        kernel.last_results = res
    return out
